# version 27
# baseline (speedup 1.0000x reference)
import numpy as np
from contextlib import ExitStack

import concourse.bass as bass
import concourse.mybir as mybir
from concourse.bass_utils import run_bass_kernel_spmd

F32 = mybir.dt.float32
BF16 = mybir.dt.bfloat16
AF = mybir.ActivationFunctionType
ALU = mybir.AluOpType

NCORE = 8
D = 1024
HID = 4096
TCORE = 2048
TT = 512
NTILE = TCORE // TT
GR = TT // 128
DC = D // 128
HC = HID // 128
NTG = TCORE // 128
EPS = 1e-5


class Builder:
    def __init__(self):
        self.nc = bass.Bass("TRN2", target_bir_lowering=False, debug=False)
        self.stacks = {}
        self.sems = {}
        self.cnt = {}
        self.hiw = {}
        self.bank_w = [None] * 8

    def stack(self, name):
        self.stacks[name] = ExitStack()
        return self.stacks[name]

    def sb(self, stack, name, shape, dt):
        return self.stacks[stack].enter_context(
            self.nc.sbuf_tensor(name, shape, dt))

    def mksem(self, *names):
        for n in names:
            self.sems[n] = self.stacks["perm"].enter_context(
                self.nc.semaphore(n))
            self.cnt[n] = 0

    def inc(self, instr, s, step=1):
        instr.then_inc(self.sems[s], step)
        self.cnt[s] += step
        return (s, self.cnt[s])

    def incD(self, instr, s):
        return self.inc(instr, s, 16)

    def tok(self, s):
        return (s, self.cnt[s])

    DEDUP = True

    def wt(self, eng, engname, *toks):
        for t in toks:
            if t is None:
                continue
            s, v = t
            if (not Builder.DEDUP) or v > self.hiw.get((engname, s), 0):
                eng.wait_ge(self.sems[s], v)
                self.hiw[(engname, s)] = v


def build_nc():
    B = Builder()
    nc = B.nc

    x_d = nc.dram_tensor("x", [TCORE, D], F32, kind="ExternalInput").ap()
    winT = nc.dram_tensor("winT", [D, D], F32, kind="ExternalInput").ap()
    waT = nc.dram_tensor("waT", [D, D], F32, kind="ExternalInput").ap()
    wbT = nc.dram_tensor("wbT", [D, D], F32, kind="ExternalInput").ap()
    wvT = nc.dram_tensor("wvT", [D, D], F32, kind="ExternalInput").ap()
    woT = nc.dram_tensor("woT", [D, D], F32, kind="ExternalInput").ap()
    w1T = nc.dram_tensor("w1T", [D, HID], F32, kind="ExternalInput").ap()
    w2T = nc.dram_tensor("w2T", [HID, D], F32, kind="ExternalInput").ap()
    w1n = nc.dram_tensor("w1n", [HID, D], F32, kind="ExternalInput").ap()
    w2n = nc.dram_tensor("w2n", [D, HID], F32, kind="ExternalInput").ap()
    bac_d = nc.dram_tensor("bac", [128, DC], F32, kind="ExternalInput").ap()
    bbc_d = nc.dram_tensor("bbc", [128, DC], F32, kind="ExternalInput").ap()
    bvc_d = nc.dram_tensor("bvc", [128, DC], F32, kind="ExternalInput").ap()
    b1c_d = nc.dram_tensor("b1c", [128, HC], F32, kind="ExternalInput").ap()
    b2r_d = nc.dram_tensor("b2r", [1, D], F32, kind="ExternalInput").ap()
    mask_d = nc.dram_tensor("mask", [128, NCORE], F32, kind="ExternalInput").ap()
    idt_d = nc.dram_tensor("idt", [128, 128], BF16, kind="ExternalInput").ap()
    y_d = nc.dram_tensor("y", [TCORE, D], F32, kind="ExternalOutput").ap()
    hsp = nc.dram_tensor("hsp", [NTILE, 128, DC * TT], F32, kind="Internal").ap()
    psp = nc.dram_tensor("psp", [NTILE, 128, DC * TT], F32, kind="Internal").ap()
    gsp = nc.dram_tensor("gsp", [NTG, 128, D], F32, kind="Internal").ap()
    osp = nc.dram_tensor("osp", [NTILE, 128, DC * TT], BF16, kind="Internal").ap()
    sw1d = nc.dram_tensor("sw1d", [HC, 128, DC, 128], BF16, kind="Internal").ap()
    sw2d = nc.dram_tensor("sw2d", [HC, 128, D], BF16, kind="Internal").ap()
    ccin = nc.dram_tensor("ccin", [128, 16], F32, kind="Internal")
    ccout = nc.dram_tensor("ccout", [NCORE * 128, 16], F32, kind="Internal")

    perm = B.stack("perm")
    B.mksem("qSet", "qWm", "qWpa", "qWpb", "qAa", "qAb", "qWs", "qRow",
            "qX0", "qX1", "qX2", "qX3",
            "qSp", "qC", "qW1a", "qW1b", "qW1c", "qW1d", "qW1e", "qW1f",
            "qW2a", "qW2b", "qW2c", "qW2d", "qGs", "qR", "qR2", "qR3",
            "qOs", "qHr", "qY", "qP", "qYa", "qYb", "qYc", "qYd", "qR4",
            "cT", "cS", "cV", "cG", "cc")
    sems, cnt = B.sems, B.cnt
    inc, incD, tok, wt = B.inc, B.incD, B.tok, B.wt
    bank_w = B.bank_w

    idt = B.sb("perm", "idts", [128, 128], BF16)
    bac = B.sb("perm", "bacs", [128, DC], F32)
    bbc = B.sb("perm", "bbcs", [128, DC], F32)
    bvc = B.sb("perm", "bvcs", [128, DC], F32)
    b1c = B.sb("perm", "b1cs", [128, HC], F32)
    a1c = B.sb("perm", "a1cs", [128, HC], F32)
    a2c8 = B.sb("perm", "a2c8s", [128, DC], F32)
    a2cb = B.sb("perm", "a2cbs", [128, DC], BF16)
    a2row8 = B.sb("perm", "a2row8s", [8, 128], F32)
    b2r = B.sb("perm", "b2rs", [1, D], F32)
    b2p = B.sb("perm", "b2ps", [1, D], BF16)
    a2acc = B.sb("perm", "a2accs", [1, D], F32)
    a2rb = B.sb("perm", "a2rbs", [1, D], BF16)
    a2bc = B.sb("perm", "a2bcs", [128, D], BF16)
    ones_u = B.sb("perm", "ones_us", [1, 128], BF16)
    ones_c = B.sb("perm", "ones_cs", [128, 1], BF16)
    rtmp = B.sb("perm", "rtmps", [128, 2], F32)
    c15 = B.sb("perm", "c15s", [128, GR], F32)
    ch = B.sb("perm", "chs", [128, GR], F32)
    ceps = B.sb("perm", "cepss", [128, GR], F32)
    mask = B.sb("perm", "masks", [128, NCORE], F32)
    gat = B.sb("perm", "gats", [128, NCORE, 16], F32)
    Hin = B.sb("perm", "Hins", [128, DC], F32)
    t1c = B.sb("perm", "t1cs", [128, DC], F32)
    P_run = B.sb("perm", "P_runs", [128, DC], F32)
    S_run = B.sb("perm", "S_runs", [128, DC], F32)
    pub = B.sb("perm", "pubs", [128, 16], F32)
    mv = B.sb("perm", "mvs", [128, 2, GR, 2], F32)
    bn6 = B.sb("perm", "bn6s", [128, 12], F32)
    wvar = B.sb("perm", "wvars", [128, 2, GR], F32)
    rva = B.sb("perm", "rvas", [128, 2, GR], F32)
    uva = B.sb("perm", "uvas", [128, 2, GR], F32)

    ps = [perm.enter_context(nc.psum_tensor(f"ps{i}", [128, 512], F32))
          for i in range(7)]

    block = perm.enter_context(nc.Block())
    T, V, Sc, Sy, G = (block.tensor, block.vector, block.scalar, block.sync,
                       block.gpsimd)

    esX = B.stack("X")
    xnf = B.sb("X", "xnf", [128, DC, TCORE], BF16)
    esA = B.stack("A")
    win_s = B.sb("A", "win_s", [128, DC, D], BF16)
    wa_s = B.sb("A", "wa_s", [128, DC, D], BF16)
    wb_s = B.sb("A", "wb_s", [128, DC, D], BF16)
    wvv_s = B.sb("A", "wvv_s", [128, DC, D], BF16)
    xfull = B.sb("A", "xfull", [128, GR, D], F32)
    xntm = B.sb("A", "xntm", [128, 2, D], BF16)
    xpf = B.sb("A", "xpf", [128, DC, TT], BF16)
    atmp = B.sb("A", "atmp", [128, 2, TT], F32)
    btmp = B.sb("A", "btmp", [128, 2, TT], F32)
    vtm2 = B.sb("A", "vtm2", [128, 2, TT], F32)
    hloc = B.sb("A", "hloc", [128, 4, TT], F32)
    ploc = B.sb("A", "ploc", [128, 4, TT], F32)
    pst = B.stacks["A"].enter_context(
        nc.psum_tensor("pst", [128, 1024], BF16))
    wfpb = B.sb("A", "wfpb", [128, 2, 2048], F32)
    signb = B.sb("A", "signb", [128, 2, 2048], BF16)

    def setup_sy(sy):
        for dst, src in ((idt, idt_d), (bac, bac_d), (bbc, bbc_d),
                         (bvc, bvc_d), (b1c, b1c_d), (b2r, b2r_d),
                         (mask, mask_d)):
            i = sy.dma_start(out=dst[:], in_=src)
            incD(i, "qSet")
    Sy(setup_sy)
    setup_tok = tok("qSet")

    def setup_pro(s):
        i = s.dma_start(out=y_d[:], in_=x_d[:])
        incD(i, "qP")
    Sc(setup_pro)
    pro_tok = tok("qP")

    def setup_g(g):
        for dst, src in ((win_s, winT), (wa_s, waT), (wb_s, wbT), (wvv_s, wvT)):
            i = g.dma_start(out=dst[:],
                            in_=src.rearrange("(kc p) o -> p kc o", p=128))
            incD(i, "qWm")
        for t_, val in ((ones_u, 1.0), (ones_c, 1.0), (a1c, 0.0),
                        (S_run, 0.0), (P_run, 1.0),
                        (Hin, 0.0), (a2c8, 0.0), (c15, 1.5), (ch, 0.5),
                        (ceps, EPS)):
            i = g.memset(t_[:], val)
            inc(i, "cG")
    G(setup_g)
    mixw_tok = tok("qWm")
    memset_tok = tok("cG")

    st = {
        "signb_free": None, "sign_prev": None,
        "a1_done": None, "prep_done": None, "a2bc_tok": None, "b2p_tok": None,
        "wfp_free": [None, None], "signb_free2": None,
        "a1_prev": None, "a2_prev": None, "b6_w": None, "w1_seen": 0,
    }

    def sign_w1_load(d, h, slot):
        def g_load(g):
            wt(g, "G", st["wfp_free"][slot])
            i = g.dma_start(out=wfpb[:, slot, :],
                            in_=w1T[d * 128:(d + 1) * 128,
                                    h * 2048:(h + 1) * 2048])
            incD(i, "qWpa" if slot == 0 else "qWpb")
        G(g_load)
        return tok("qWpa" if slot == 0 else "qWpb")

    def sign_w2_load(m, slot):
        def g_load(g):
            wt(g, "G", st["wfp_free"][slot])
            i = g.dma_start(
                out=wfpb[:, slot, :].rearrange("p (c d) -> p c d", c=2),
                in_=w2T[m * 256:(m + 1) * 256, :].rearrange(
                    "(c p) d -> p c d", p=128))
            incD(i, "qWpa" if slot == 0 else "qWpb")
        G(g_load)
        return tok("qWpa" if slot == 0 else "qWpb")

    def sign_compute_pair(items):
        lds = [it[4] for it in items]

        def s_sign(s):
            wt(s, "S", st["signb_free"], st["signb_free2"], st["sign_prev"],
               *lds)
            for kind, a_, h, slot, ld in items:
                i = s.activation(out=signb[:, slot, :], in_=wfpb[:, slot, :],
                                 func=AF.Sign)
                inc(i, "cS")
        Sc(s_sign)
        sg_t = tok("cS")
        st["sign_prev"] = sg_t

        def g_store(g):
            wt(g, "G", sg_t)
            for kind, a_, h, slot, ld in items:
                if kind == "w1":
                    ci0 = h * 16
                    i = g.dma_start(
                        out=sw1d[ci0:ci0 + 16, :, a_, :].rearrange(
                            "ci p c -> p ci c"),
                        in_=signb[:, slot, :].rearrange(
                            "p (ci c) -> p ci c", ci=16))
                else:
                    i = g.dma_start(
                        out=sw2d[2 * a_:2 * a_ + 2, :, :].rearrange(
                            "ci p d -> p ci d"),
                        in_=signb[:, slot, :].rearrange(
                            "p (ci d) -> p ci d", ci=2))
                incD(i, "qWs")
        G(g_store)
        st["signb_free2"] = tok("qWs")

        def g_abs(g):
            wt(g, "G", tok("qWs"))
            for kind, a_, h, slot, ld in items:
                i = g.tensor_mul(signb[:, slot, :], wfpb[:, slot, :],
                                  signb[:, slot, :])
                inc(i, "cG")
        G(g_abs)
        ab_t = tok("cG")
        for kind, a_, h, slot, ld in items:
            st["wfp_free"][slot] = ab_t

        def t_mini(t, ab_t=ab_t):
            wt(t, "T", ab_t, st["b6_w"], setup_tok)
            for it_i, (kind, a_, h, slot, ld) in enumerate(items):
                i = None
                for q in range(16):
                    i = t.matmul(ps[6][:, it_i * 16 + q:it_i * 16 + q + 1],
                                 signb[:, slot, q * 128:(q + 1) * 128],
                                 ones_c[:], start=True, stop=True)
                inc(i, "cT")
        T(t_mini)
        st["signb_free"] = tok("cT")

        def v_acc(v):
            wt(v, "V", tok("cT"), memset_tok)
            for it_i, (kind, a_, h, slot, ld) in enumerate(items):
                b0 = it_i * 16
                if kind == "w1":
                    i = v.tensor_add(a1c[:, h * 16:(h + 1) * 16],
                                     a1c[:, h * 16:(h + 1) * 16],
                                     ps[6][:, b0:b0 + 16])
                    inc(i, "cV")
                else:
                    wt(v, "V", st["a2_prev"])
                    i = v.tensor_add(a2c8[:], a2c8[:], ps[6][:, b0:b0 + 8])
                    inc(i, "cV")
                    wt(v, "V", tok("cV"))
                    i = v.tensor_add(a2c8[:], a2c8[:],
                                     ps[6][:, b0 + 8:b0 + 16])
                    inc(i, "cV")
                    st["a2_prev"] = tok("cV")
        V(v_acc)
        st["b6_w"] = tok("cV")
        st["a1_prev"] = tok("cV")
        st["w1_seen"] += sum(1 for it in items if it[0] == "w1")
        if st["w1_seen"] == 16 and st["a1_done"] is None:
            def v_sc1(v):
                wt(v, "V", tok("cV"))
                i = v.tensor_scalar(out=a1c[:], in0=a1c[:],
                                    scalar1=1.0 / D, scalar2=None,
                                    op0=ALU.mult)
                inc(i, "cV")
            V(v_sc1)
            st["a1_done"] = tok("cV")

    def alpha2_finalize():
        def v_sc(v):
            wt(v, "V", st["a2_prev"])
            i = v.tensor_scalar(out=a2c8[:], in0=a2c8[:], scalar1=1.0 / HID,
                                scalar2=None, op0=ALU.mult)
            inc(i, "cV")
        V(v_sc)
        sc_t = tok("cV")

        def s_cast(s):
            wt(s, "S", sc_t)
            i = s.copy(out=a2cb[:], in_=a2c8[:])
            inc(i, "cS")
        Sc(s_cast)
        cb_t = tok("cS")

        def t_tr(t):
            wt(t, "T", cb_t, A["pst_free"])
            i = t.transpose(pst[0:8, 0:128], a2cb[:], idt[:])
            inc(i, "cT")
        T(t_tr)
        tr_t = tok("cT")

        def s_row(s):
            wt(s, "S", tr_t)
            i = s.copy(out=a2row8[:], in_=pst[0:8, 0:128])
            inc(i, "cS")
        Sc(s_row)
        A["pst_free"] = tok("cS")
        row_t = tok("cS")

        def sy_row(sy):
            wt(sy, "Y", row_t)
            for r in range(8):
                i = sy.dma_start(out=a2acc[0:1, r * 128:(r + 1) * 128],
                                 in_=a2row8[r:r + 1, :])
                incD(i, "qRow")
        Sy(sy_row)
        acc_t = tok("qRow")

        def s_rb(s):
            wt(s, "S", acc_t)
            i = s.copy(out=a2rb[:], in_=a2acc[:])
            inc(i, "cS")
        Sc(s_rb)
        rb_t = tok("cS")

        def v_b2p(v):
            wt(v, "V", rb_t)
            i = v.reciprocal(out=a2acc[:], in_=a2acc[:])
            inc(i, "cV")
            wt(v, "V", tok("cV"))
            i = v.tensor_mul(a2acc[:], a2acc[:], b2r[:])
            inc(i, "cV")
        V(v_b2p)
        bp_t = tok("cV")

        def s_b2p(s):
            wt(s, "S", bp_t)
            i = s.copy(out=b2p[:], in_=a2acc[:])
            inc(i, "cS")
        Sc(s_b2p)
        st["b2p_tok"] = tok("cS")

        for sg in range(2):
            def t_bc(t, sg=sg):
                wt(t, "T", tok("cS"), bank_w[6])
                i = t.matmul(ps[6][:, :], ones_u[:],
                             a2rb[0:1, sg * 512:(sg + 1) * 512],
                             start=True, stop=True)
                inc(i, "cT")
            T(t_bc)
            bc_t = tok("cT")

            def s_bc(s, sg=sg, bc_t=bc_t):
                wt(s, "S", bc_t)
                i = s.copy(out=a2bc[:, sg * 512:(sg + 1) * 512],
                           in_=ps[6][:, :])
                inc(i, "cS")
            Sc(s_bc)
            bank_w[6] = tok("cS")
        st["a2bc_tok"] = tok("cS")
        st["prep_done"] = tok("qWs")

    A = {
        "apply_done": [None, None], "trans_done": [None, None],
        "stat_done": [None, None], "stats_last": [None, None],
        "sgt_free": [None, None], "abv_free": [None, None],
        "vtmp_free": None, "spill": [None] * NTILE,
        "scan_last": None, "rot": 0, "spill_h": [None, None],
        "apply_prev": [None, None], "sgt_prev": [None, None],
        "apply_g": [None] * GR, "x_ld": [None] * GR,
        "xp_e": [None] * DC,
        "sqrt_done": [None, None],
        "recip_prev": [None, None], "rsqrt_tok": None,
        "abv_prev": [None, None], "xpf_free": None,
        "xnf_tok": [None] * NTILE, "pst_free": None,
    }

    def a_ln_load(j, g_):
        gi = j * GR + g_

        def sy_load(sy, g_=g_, gi=gi):
            wt(sy, "Y", A["apply_g"][g_])
            i = sy.dma_start(out=xfull[:, g_, :],
                             in_=x_d[gi * 128:(gi + 1) * 128, :])
            incD(i, "qX%d" % g_)
        Sy(sy_load)
        A["x_ld"][g_] = tok("qX%d" % g_)

    def a_ln_stats(j, g_):
        tp = j % 2
        ld = A["x_ld"][g_]

        def v_stats(v, g_=g_, tp=tp, ld=ld):
            wt(v, "V", ld, memset_tok, A["sqrt_done"][tp])
            wt(v, "V", tok("cV"))
            v.bn_stats(out=bn6[:, 0:6], in_=xfull[:, g_, 0:512])
            i = v.bn_stats(out=bn6[:, 6:12], in_=xfull[:, g_, 512:1024])
            inc(i, "cV")
            wt(v, "V", tok("cV"))
            i = v.bn_aggr(out=mv[:, tp, g_, :], in_=bn6[:])
            inc(i, "cV")
        V(v_stats)
        A["stats_last"][tp] = tok("cV")

    def a_ln_rsqrt(j):
        tp = j % 2
        stats_last = A["stats_last"][tp]

        def s_sqv(s, tp=tp, stats_last=stats_last):
            wt(s, "S", stats_last, memset_tok, A["recip_prev"][tp])
            i = s.activation(out=uva[:, tp, :], in_=mv[:, tp, :, 1],
                             func=AF.Sqrt, bias=ceps[:, 0:1])
            inc(i, "cS")
        Sc(s_sqv)
        A["sqrt_done"][tp] = tok("cS")

        def v_rec(v, tp=tp):
            wt(v, "V", A["sqrt_done"][tp], tok("cV"))
            i = v.reciprocal(rva[:, tp, :], uva[:, tp, :])
            inc(i, "cV")
        V(v_rec)
        A["rsqrt_tok"] = tok("cV")
        A["recip_prev"][tp] = A["rsqrt_tok"]

    def a_ln_apply(j):
        tp = j % 2
        last_ev = None
        for g_ in range(GR):
            sl = g_ % 2

            def v_apply(v, g_=g_, sl=sl, tp=tp):
                wt(v, "V", A["rsqrt_tok"], A["trans_done"][sl],
                   A["apply_prev"][sl])
                i = v.tensor_scalar(out=xntm[:, sl, :], in0=xfull[:, g_, :],
                                    scalar1=mv[:, tp, g_, 0:1],
                                    scalar2=rva[:, tp, g_:g_ + 1],
                                    op0=ALU.subtract, op1=ALU.mult)
                inc(i, "cV")
            V(v_apply)
            ap_t = tok("cV")
            A["apply_prev"][sl] = ap_t
            A["apply_done"][sl] = ap_t
            A["apply_g"][g_] = ap_t

            def t_trans(t, sl=sl, ap_t=ap_t):
                wt(t, "T", ap_t, A["pst_free"], setup_tok)
                i = None
                for kc in range(DC):
                    i = t.transpose(
                        pst[:, kc * 128:(kc + 1) * 128],
                        xntm[:, sl, kc * 128:(kc + 1) * 128], idt[:])
                inc(i, "cT")
            T(t_trans)
            tr_t = tok("cT")

            def v_evac(v, j=j, g_=g_, tr_t=tr_t):
                wt(v, "V", tr_t)
                off = j * TT + g_ * 128
                i = v.tensor_copy(xnf[:, :, off:off + 128],
                                  pst[:].rearrange("p (k c) -> p k c", k=DC))
                inc(i, "cV")
            V(v_evac)
            A["pst_free"] = tok("cV")
            A["trans_done"][sl] = tok("cT")
            last_ev = tok("cV")
        A["xnf_tok"][j] = last_ev

    def a_mm(j, preps=(), lnj=None):
        preps = list(preps)

        def run_prep():
            if preps:
                preps.pop(0)()

        xnf_tok = A["xnf_tok"][j]
        for ep in range(DC // 2):
            mm_ts = []
            for e in (2 * ep, 2 * ep + 1):
                bank = e % 6

                def t_xp(t, e=e, bank=bank, j=j, xnf_tok=xnf_tok):
                    wt(t, "T", bank_w[bank], xnf_tok, mixw_tok)
                    i = None
                    for kc in range(DC):
                        i = t.matmul(ps[bank][:],
                                     win_s[:, kc, e * 128:(e + 1) * 128],
                                     xnf[:, kc, j * TT:(j + 1) * TT],
                                     start=(kc == 0), stop=(kc == DC - 1))
                    inc(i, "cT")
                T(t_xp)
                mm_ts.append(tok("cT"))

            def s_silu(s, ep=ep, mm_ts=mm_ts):
                wt(s, "S", mm_ts[0], A["xpf_free"], A["sgt_prev"][1])
                for ix, e in enumerate((2 * ep, 2 * ep + 1)):
                    if ix == 1:
                        wt(s, "S", mm_ts[1])
                    i = s.activation(out=xpf[:, e, :],
                                     in_=ps[e % 6][:], func=AF.Silu)
                    inc(i, "cS")
                    bank_w[e % 6] = tok("cS")
                    A["xp_e"][e] = tok("cS")
            Sc(s_silu)
            A["sgt_prev"][1] = tok("cS")
            run_prep()

        if lnj is not None:
            for g_ in range(GR):
                a_ln_load(lnj, g_)
            for g_ in range(GR):
                a_ln_stats(lnj, g_)

        for fp in range(DC // 2):
            fbanks = []
            mmts = []
            for f in (2 * fp, 2 * fp + 1):
                ba = A["rot"] % 6
                bb_ = (A["rot"] + 1) % 6
                bv_ = (A["rot"] + 2) % 6
                A["rot"] += 3
                fbanks.append((ba, bb_, bv_))

                def t_abv(t, f=f, ba=ba, bb_=bb_, bv_=bv_):
                    for wsb, bank in ((wa_s, ba), (wb_s, bb_), (wvv_s, bv_)):
                        wt(t, "T", bank_w[bank])
                        i = None
                        for e in range(DC):
                            wt(t, "T", A["xp_e"][e])
                            i = t.matmul(ps[bank][:],
                                         wsb[:, e, f * 128:(f + 1) * 128],
                                         xpf[:, e, :],
                                         start=(e == 0), stop=(e == DC - 1))
                        inc(i, "cT")
                T(t_abv)
                mmts.append(tok("cT"))

            def s_abv(s, fp=fp, fbanks=fbanks, mmts=mmts):
                abv2 = {}
                for sl in range(2):
                    f = 2 * fp + sl
                    ba, bb_, bv_ = fbanks[sl]
                    wt(s, "S", mmts[sl], A["abv_free"][sl],
                       A["abv_prev"][sl])
                    i = s.activation(out=atmp[:, sl, :], in_=ps[ba][:],
                                     func=AF.Sigmoid, bias=bac[:, f:f + 1])
                    inc(i, "cS")
                    bank_w[ba] = tok("cS")
                    i = s.activation(out=btmp[:, sl, :], in_=ps[bb_][:],
                                     func=AF.Sigmoid, bias=bbc[:, f:f + 1])
                    inc(i, "cS")
                    bank_w[bb_] = tok("cS")
                    i = s.activation(out=vtm2[:, sl, :], in_=ps[bv_][:],
                                     func=AF.Tanh, bias=bvc[:, f:f + 1])
                    inc(i, "cS")
                    bank_w[bv_] = tok("cS")
                    abv2[sl] = tok("cS")
                    A["abv_prev"][sl] = abv2[sl]
                A["abv2"] = abv2
            Sc(s_abv)
            abv2 = A["abv2"]

            for f in (2 * fp, 2 * fp + 1):
                slot = f % 2
                abv_t = abv2[slot]

                def v_scan(v, f=f, slot=slot, abv_t=abv_t):
                    wt(v, "V", abv_t, A["spill_h"][1 - f // 4])
                    i = v.tensor_mul(btmp[:, slot, :], btmp[:, slot, :],
                                     vtm2[:, slot, :])
                    inc(i, "cV")
                    wt(v, "V", tok("cV"))
                    i = v.tensor_tensor_scan(out=hloc[:, f % 4, :],
                                             data0=atmp[:, slot, :],
                                             data1=btmp[:, slot, :],
                                             initial=S_run[:, f:f + 1],
                                             op0=ALU.mult, op1=ALU.add)
                    inc(i, "cV")
                    i = v.tensor_tensor_scan(out=ploc[:, f % 4, :],
                                             data0=atmp[:, slot, :],
                                             data1=atmp[:, slot, :],
                                             initial=P_run[:, f:f + 1],
                                             op0=ALU.mult, op1=ALU.min)
                    inc(i, "cV")
                V(v_scan)
                A["abv_free"][slot] = tok("cV")
                if f % 4 == 3:
                    hh = f // 4

                    def v_sth(v, hh=hh):
                        wt(v, "V", tok("cV"))
                        i = v.tensor_copy(S_run[:, hh * 4:hh * 4 + 4],
                                          hloc[:, :, TT - 1])
                        inc(i, "cV")
                        i = v.tensor_copy(P_run[:, hh * 4:hh * 4 + 4],
                                          ploc[:, :, TT - 1])
                        inc(i, "cV")
                    V(v_sth)
                    sc = tok("cV")
                    if hh == 1:
                        A["scan_last"] = sc

                    def g_spill(g, j=j, hh=hh, sc=sc):
                        wt(g, "G", sc)
                        i = g.dma_start(
                            out=hsp[j, :, hh * 2048:(hh + 1) * 2048],
                            in_=hloc[:].rearrange("p a b -> p (a b)"))
                        incD(i, "qSp")
                        i = g.dma_start(
                            out=psp[j, :, hh * 2048:(hh + 1) * 2048],
                            in_=ploc[:].rearrange("p a b -> p (a b)"))
                        incD(i, "qSp")
                    G(g_spill)
                    A["spill_h"][hh] = tok("qSp")
                    if hh == 1:
                        A["spill"][j] = A["spill_h"][hh]
            if lnj is not None and fp == 1:
                a_ln_rsqrt(lnj)
            run_prep()
            run_prep()

        A["xpf_free"] = tok("cT")

        if lnj is not None:
            a_ln_apply(lnj)

    sign_slices = [("w1", d, h) for d in range(DC) for h in range(2)] + \
                  [("w2", m, 0) for m in range(16)]
    pstate = {"sp": []}

    def prep_point(k):
        sp = pstate["sp"]
        if len(sp) >= 2 and (k % 2 == 0 or k >= 32):
            items = [sp.pop(0), sp.pop(0)]
            sign_compute_pair(items)
        elif k >= 32 and sp:
            sign_compute_pair([sp.pop(0)])
        if k < 32:
            kind, a_, h = sign_slices[k]
            slot = k % 2
            ld = (sign_w1_load(a_, h, slot) if kind == "w1"
                  else sign_w2_load(a_, slot))
            sp.append((kind, a_, h, slot, ld))

    for g_ in range(GR):
        a_ln_load(0, g_)
    for g_ in range(GR):
        a_ln_stats(0, g_)
    a_ln_rsqrt(0)
    a_ln_apply(0)
    pp_off = [0, 11, 22, 32, 32]
    for j in range(NTILE):
        a_mm(j, [(lambda k: (lambda: prep_point(k)))(pp_off[j] + q)
                 for q in range(pp_off[j + 1] - pp_off[j])],
             lnj=j + 1 if j + 1 < NTILE else None)
    k = 32
    while pstate["sp"]:
        prep_point(k)
        k += 1
    alpha2_finalize()

    def v_pub(v):
        wt(v, "V", A["scan_last"])
        i = v.tensor_copy(pub[:, 0:8], P_run[:])
        inc(i, "cV")
        i = v.tensor_copy(pub[:, 8:16], S_run[:])
        inc(i, "cV")
    V(v_pub)
    pub_tok = tok("cV")

    def sy_pub(sy):
        wt(sy, "Y", pub_tok)
        i = sy.dma_start(out=ccin.ap(), in_=pub[:])
        incD(i, "qC")
    Sy(sy_pub)
    ccin_tok = tok("qC")

    def g_cc(g):
        wt(g, "G", ccin_tok)
        i = g.collective_compute(
            "AllGather", ALU.bypass,
            replica_groups=[list(range(NCORE))],
            ins=[ccin.ap()], outs=[ccout.ap()])
        inc(i, "cc")
    G(g_cc)
    cc_tok = tok("cc")

    def sy_gat(sy):
        wt(sy, "Y", cc_tok)
        i = sy.dma_start(out=gat[:],
                         in_=ccout.ap().rearrange("(r p) c -> p r c", p=128))
        incD(i, "qC")
    Sy(sy_gat)
    gat_tok = tok("qC")

    def v_compose(v):
        wt(v, "V", gat_tok, memset_tok, setup_tok)
        for jr in range(NCORE):
            wt(v, "V", tok("cV"))
            i = v.tensor_mul(t1c[:], gat[:, jr, 0:8], Hin[:])
            inc(i, "cV")
            wt(v, "V", tok("cV"))
            i = v.tensor_add(t1c[:], t1c[:], gat[:, jr, 8:16])
            inc(i, "cV")
            wt(v, "V", tok("cV"))
            i = v.tensor_sub(t1c[:], t1c[:], Hin[:])
            inc(i, "cV")
            wt(v, "V", tok("cV"))
            i = v.scalar_tensor_tensor(out=Hin[:], in0=t1c[:],
                                       scalar=mask[:, jr:jr + 1], in1=Hin[:],
                                       op0=ALU.mult, op1=ALU.add)
            inc(i, "cV")
    V(v_compose)
    hin_tok = tok("cV")

    aend = (A["spill"][NTILE - 1], A["scan_last"], tok("cT"), tok("cS"),
            st["prep_done"])
    esA.close()
    esH = B.stack("H")
    ps.append(esH.enter_context(nc.psum_tensor("ps7", [128, 512], F32)))
    h1 = B.sb("H", "h1", [128, HC, TCORE], BF16)
    esC = B.stack("C")
    hlo = B.sb("C", "hlo", [128, DC, 256], F32)
    plo = B.sb("C", "plo", [128, DC, 256], F32)
    hht = B.sb("C", "hht", [128, 256], F32)
    otst = B.sb("C", "otst", [128, DC, 256], BF16)

    C = {"hp_free": None, "ot_free": None, "th_prev": None, "hh_free": None}

    def corr_unit(u):
        j, hh = u // 2, u % 2
        c0 = hh * 256

        def sy_rel(sy, j=j, c0=c0, u=u):
            wt(sy, "Y", C["hp_free"], tok("qSp"))
            if u == 0:
                wt(sy, "Y", st["prep_done"], st["sign_prev"],
                   st["a1_prev"], st["a2_prev"], A["scan_last"])
            i = sy.dma_start(
                out=hlo[:],
                in_=hsp[j, :, :].rearrange("p (f t) -> p f t", f=DC)[
                    :, :, c0:c0 + 256])
            incD(i, "qHr")
            i = sy.dma_start(
                out=plo[:],
                in_=psp[j, :, :].rearrange("p (f t) -> p f t", f=DC)[
                    :, :, c0:c0 + 256])
            incD(i, "qHr")
        Sy(sy_rel)
        hp_t = tok("qHr")

        def v_corr(v, hp_t=hp_t):
            wt(v, "V", hp_t, hin_tok, C["hh_free"])
            for f in range(DC):
                i = v.scalar_tensor_tensor(out=hlo[:, f, :], in0=plo[:, f, :],
                                           scalar=Hin[:, f:f + 1],
                                           in1=hlo[:, f, :],
                                           op0=ALU.mult, op1=ALU.add)
                inc(i, "cV")
        V(v_corr)
        corr_t = tok("cV")
        C["hp_free"] = corr_t

        def s_th(s, corr_t=corr_t):
            wt(s, "S", corr_t, C["th_prev"], C["ot_free"])
            for f in range(DC):
                i = s.activation(out=otst[:, f, :], in_=hlo[:, f, :],
                                 func=AF.Tanh, scale=0.5)
                inc(i, "cS")
        Sc(s_th)
        th_t = tok("cS")
        C["th_prev"] = th_t

        def v_o(v, th_t=th_t):
            wt(v, "V", th_t)
            for f in range(DC):
                wt(v, "V", tok("cV"))
                i = v.tensor_mul(hht[:], hlo[:, f, :], otst[:, f, :])
                inc(i, "cV")
                wt(v, "V", tok("cV"))
                i = v.tensor_add(hht[:], hht[:], hlo[:, f, :])
                inc(i, "cV")
                wt(v, "V", tok("cV"))
                i = v.scalar_tensor_tensor(out=otst[:, f, :],
                                           in0=hht[:], scalar=0.5,
                                           in1=hlo[:, f, :],
                                           op0=ALU.mult, op1=ALU.mult)
                inc(i, "cV")
        V(v_o)
        C["hh_free"] = tok("cV")
        o_done = tok("cV")
        o_done = tok("cV")

        def sy_osp(sy, j=j, c0=c0, o_done=o_done):
            wt(sy, "Y", o_done)
            i = sy.dma_start(
                out=osp[j, :, :].rearrange("p (f t) -> p f t", f=DC)[
                    :, :, c0:c0 + 256],
                in_=otst[:])
            incD(i, "qOs")
        Sy(sy_osp)
        C["ot_free"] = tok("qOs")
    esS1 = B.stack("S1")
    sw1r = B.sb("S1", "sw1r", [128, 5, DC, 128], BF16)

    sw1_free = [None] * 5
    W1Q = ["qW1a", "qW1b", "qW1c", "qW1d", "qW1e"]
    gelu_toks = []
    for ci in range(HC):
        if ci % 4 == 0:
            corr_unit(ci // 4)
        slot = ci % 5

        def g_ld(g, ci=ci, slot=slot):
            if ci < 5:
                wt(g, "G", *aend)
            wt(g, "G", sw1_free[slot], st["prep_done"])
            i = g.dma_start(out=sw1r[:, slot, :, :], in_=sw1d[ci, :, :, :])
            incD(i, W1Q[slot])
        G(g_ld)
        ld = tok(W1Q[slot])

        def t_h1(t, ci=ci, slot=slot, ld=ld):
            wt(t, "T", ld, st["a1_done"])
            for jt in range(NTILE):
                wt(t, "T", bank_w[(ci % 2) * 4 + jt])
            i = None
            for d in range(DC):
                for jt in range(NTILE):
                    i = t.matmul(ps[(ci % 2) * 4 + jt][:],
                                 sw1r[:, slot, d, :],
                                 xnf[:, d, jt * TT:(jt + 1) * TT],
                                 start=(d == 0), stop=(d == DC - 1))
                    if d == DC - 1:
                        inc(i, "cT")
        T(t_h1)
        mm_end = cnt["cT"]

        def s_gelu(s, ci=ci, mm_end=mm_end):
            for jt in range(NTILE):
                wt(s, "S", ("cT", mm_end - (NTILE - 1 - jt)))
                i = s.activation(out=h1[:, ci, jt * TT:(jt + 1) * TT],
                                 in_=ps[(ci % 2) * 4 + jt][:], func=AF.Gelu,
                                 scale=a1c[:, ci:ci + 1], bias=b1c[:, ci:ci + 1])
                inc(i, "cS")
                bank_w[(ci % 2) * 4 + jt] = tok("cS")
        Sc(s_gelu)
        sw1_free[slot] = tok("cS")
        gelu_toks.append(tok("cS"))
    h1_tok = gelu_toks[-1]

    b1a_end = tok("cT")
    esS1.close()
    esC.close()
    corr_done = (C["ot_free"], tok("cG"))
    esS2 = B.stack("S2")
    sw2r = B.sb("S2", "sw2r", [128, 4, D], BF16)
    gst = B.sb("S2", "gst", [128, GR, D], F32)

    sw2_free = [(b1a_end, *corr_done)] * 4
    sw2_free = [b1a_end] * 4
    cd0, cd1 = corr_done
    W2Q = ["qW2a", "qW2b", "qW2c", "qW2d"]
    gst_free = None

    gt2b = xnf.bitcast(F32).reshape([128, 2, GR, D])
    rel2_tok = [None] * NTILE
    g_done = [None, None]

    def b2_rel2(j, war):
        sl = j % 2
        q = "qR2" if sl == 0 else "qR4"

        def sy_rel2(sy, j=j, sl=sl, q=q, war=war):
            wt(sy, "Y", b1a_end, war)
            i = sy.dma_start(out=gt2b[:, sl, :, :],
                             in_=gsp[j * GR:(j + 1) * GR, :, :].rearrange(
                                 "t p c -> p t c"))
            incD(i, q)
        Sy(sy_rel2)
        rel2_tok[j] = tok(q)
    for tgg in range(4):
        def t_ones(t, tgg=tgg):
            wt(t, "T", st["b2p_tok"], h1_tok)
            for k in range(8):
                wt(t, "T", bank_w[k])
            for tg4 in range(4):
                for hf in range(2):
                    t.matmul(ps[tg4 * 2 + hf][:], ones_u[:],
                             b2p[0:1, hf * 512:(hf + 1) * 512],
                             start=True, stop=False)
        T(t_ones)

        for ci in range(HC):
            slot = ci % 4

            def g_ld(g, ci=ci, slot=slot):
                wt(g, "G", sw2_free[slot], cd0, cd1)
                i = g.dma_start(out=sw2r[:, slot, :], in_=sw2d[ci, :, :])
                incD(i, W2Q[slot])
            G(g_ld)
            ld = tok(W2Q[slot])

            def t_w2(t, tgg=tgg, ci=ci, slot=slot, ld=ld):
                wt(t, "T", ld)
                i = None
                for tg4 in range(4):
                    tg = tgg * 4 + tg4
                    for hf in range(2):
                        i = t.matmul(
                            ps[tg4 * 2 + hf][:],
                            h1[:, ci, tg * 128:(tg + 1) * 128],
                            sw2r[:, slot, hf * 512:(hf + 1) * 512],
                            start=False, stop=(ci == HC - 1))
                        if ci == HC - 1 or (tg4 == 3 and hf == 1):
                            inc(i, "cT")
            T(t_w2)
            sw2_free[slot] = tok("cT")
        w2_end = cnt["cT"]

        def v_gev(v, tgg=tgg, w2_end=w2_end, gf=gst_free):
            wt(v, "V", ("cT", w2_end), st["a2bc_tok"], gf)
            for tg4 in range(4):
                for hf in range(2):
                    i = v.tensor_mul(gst[:, tg4, hf * 512:(hf + 1) * 512],
                                     ps[tg4 * 2 + hf][:],
                                     a2bc[:, hf * 512:(hf + 1) * 512])
                    inc(i, "cV")
                    bank_w[tg4 * 2 + hf] = tok("cV")
        V(v_gev)
        gev_t = tok("cV")

        def sy_gsp(sy, tgg=tgg, gev_t=gev_t):
            wt(sy, "Y", gev_t)
            for tg4 in range(4):
                i = sy.dma_start(out=gsp[tgg * 4 + tg4, :, :],
                                 in_=gst[:, tg4, :])
                incD(i, "qGs")
        Sy(sy_gsp)
        gst_free = tok("qGs")
        if tgg < 2:
            b2_rel2(tgg, gst_free)

    b1b_end = tok("cT")
    esS2.close()
    esH.close()
    esB2 = B.stack("B2")
    wo_s = B.sb("B2", "wo_s", [128, DC, D], BF16)

    def g_wo(g):
        wt(g, "G", tok("cT"))
        i = g.dma_start(out=wo_s[:],
                        in_=woT.rearrange("(kc p) o -> p kc o", p=128))
        incD(i, "qWm")
    G(g_wo)
    wo_tok = tok("qWm")
    otb = B.sb("B2", "otb", [128, 2, DC, TT], BF16)
    gs2 = B.sb("B2", "gs2", [128, 2, 512], F32)
    sv2 = B.sb("B2", "sv2", [128, 4, 2, 512], F32)

    b2tok = {
        "ot_free": [b1b_end, b1b_end],
        "g_free": gst_free,
        "gs_free": [b1b_end, b1b_end], "sv_free": [None, None],
        "gs_prev": b1b_end, "acc_done": [None] * 4,
    }
    ot_tok = [None] * NTILE
    final_toks = []

    def b2_rel(j):
        sl = j % 2
        q = "qR" if sl == 0 else "qR3"

        def sy_rel(sy, j=j, sl=sl, q=q):
            wt(sy, "Y", b2tok["ot_free"][sl])
            if j == 1:
                wt(sy, "Y", ot_tok[0][1])
            if j < 2:
                wt(sy, "Y", b2tok["g_free"])
                wt(sy, "Y", tok("qOs"))
            if j == 0:
                i = sy.dma_start(out=otb[:, sl, 0:4, :].rearrange(
                                     "p a b -> p (a b)"),
                                 in_=osp[j, :, 0:2048])
                incD(i, "qR")
                i = sy.dma_start(out=otb[:, sl, 4:8, :].rearrange(
                                     "p a b -> p (a b)"),
                                 in_=osp[j, :, 2048:4096])
                incD(i, "qR3")
            else:
                i = sy.dma_start(out=otb[:, sl, :, :].rearrange(
                                     "p a b -> p (a b)"),
                                 in_=osp[j, :, :])
                incD(i, q)
        Sy(sy_rel)
        if j == 0:
            ot_tok[j] = (tok("qR"), tok("qR3"))
        else:
            ot_tok[j] = (tok(q), tok(q))

    def b2_tile(j):
        sl = j % 2
        rel_t = rel2_tok[j]

        for tg4 in range(GR):
            tg = j * GR + tg4
            mm_ts = {}
            for hf in range(2):
                bank = (tg4 % 2) * 2 + hf

                def t_s(t, j=j, sl=sl, tg4=tg4, hf=hf, bank=bank):
                    wt(t, "T", ot_tok[j][0], wo_tok, bank_w[bank])
                    i = None
                    for f in range(DC):
                        if f == 4:
                            wt(t, "T", ot_tok[j][1])
                        i = t.matmul(ps[bank][:],
                                     otb[:, sl, f, tg4 * 128:(tg4 + 1) * 128],
                                     wo_s[:, f, hf * 512:(hf + 1) * 512],
                                     start=(f == 0), stop=(f == DC - 1))
                    inc(i, "cT")
                T(t_s)
                mm_ts[hf] = tok("cT")

            def s_gs(s, sl=sl, tg4=tg4, rel_t=rel_t):
                wt(s, "S", rel_t, b2tok["gs_free"][0], b2tok["gs_free"][1],
                   b2tok["gs_prev"])
                for hf in range(2):
                    i = s.activation(out=gs2[:, hf, :],
                                     in_=gt2b[:, sl, tg4,
                                              hf * 512:(hf + 1) * 512],
                                     func=AF.Sigmoid)
                    inc(i, "cS")
            Sc(s_gs)
            gs_t = tok("cS")
            b2tok["gs_prev"] = gs_t

            tpar = tg4 % 4
            for hf in range(2):
                bank = (tg4 % 2) * 2 + hf

                def v_fin(v, sl=sl, tg4=tg4, tpar=tpar, hf=hf, bank=bank,
                          s_mm=mm_ts[hf], gs_t=gs_t):
                    wt(v, "V", gs_t, s_mm, b2tok["acc_done"][tpar])
                    i = v.tensor_mul(gt2b[:, sl, tg4,
                                          hf * 512:(hf + 1) * 512],
                                     gt2b[:, sl, tg4,
                                          hf * 512:(hf + 1) * 512],
                                     gs2[:, hf, :])
                    inc(i, "cV")
                    b2tok["gs_free"][hf] = tok("cV")
                    wt(v, "V", tok("cV"))
                    i = v.tensor_mul(sv2[:, tpar, hf, :], ps[bank][:],
                                     gt2b[:, sl, tg4,
                                          hf * 512:(hf + 1) * 512])
                    inc(i, "cV")
                    bank_w[bank] = tok("cV")
                V(v_fin)

            fin_t = tok("cV")
            qacc = ("qYa", "qYb", "qYc", "qYd")[tpar]

            def g_acc(g, tg=tg, tpar=tpar, fin_t=fin_t, qacc=qacc):
                wt(g, "G", fin_t, pro_tok)
                i = g.dma_start(out=y_d[tg * 128:(tg + 1) * 128, :],
                                in_=sv2[:, tpar, :, :].rearrange(
                                    "p a b -> p (a b)"),
                                accum_op=ALU.add)
                incD(i, qacc)
            G(g_acc)
            b2tok["acc_done"][tpar] = tok(qacc)
            final_toks.append(tok(qacc))
        b2tok["ot_free"][sl] = tok("cT")
        b2tok["g_free"] = tok("cV")
        g_done[sl] = tok("cV")

    b2_rel(0)
    for j in range(NTILE):
        if j + 1 < NTILE:
            b2_rel(j + 1)
        b2_tile(j)
        if j + 2 < NTILE:
            b2_rel2(j + 2, g_done[j % 2])

    def sy_end(sy):
        wt(sy, "Y", pro_tok, *final_toks[-4:])
    Sy(sy_end)

    esB2.close()
    esX.close()
    B.stacks["perm"].close()
    nc._dbg_cnt = dict(cnt)
    return nc


_NC_CACHE = {}


def kernel(**inputs):
    x = np.ascontiguousarray(np.asarray(inputs["L0_attn_x"], dtype=np.float32))
    W_in = np.asarray(inputs["L0_attn_W_in"], dtype=np.float32)
    W_a = np.asarray(inputs["L0_attn_W_alpha"], dtype=np.float32)
    b_a = np.asarray(inputs["L0_attn_b_alpha"], dtype=np.float32)
    W_b = np.asarray(inputs["L0_attn_W_beta"], dtype=np.float32)
    b_b = np.asarray(inputs["L0_attn_b_beta"], dtype=np.float32)
    W_v = np.asarray(inputs["L0_attn_W_v"], dtype=np.float32)
    b_v = np.asarray(inputs["L0_attn_b_v"], dtype=np.float32)
    W_o = np.asarray(inputs["L0_attn_W_out"], dtype=np.float32)
    w1 = np.asarray(inputs["L0_ffn_w1"], dtype=np.float32)
    b1 = np.asarray(inputs["L0_ffn_b1"], dtype=np.float32)
    w2 = np.asarray(inputs["L0_ffn_w2"], dtype=np.float32)
    b2 = np.asarray(inputs["L0_ffn_b2"], dtype=np.float32)

    Bn, Tn, Dn = x.shape
    assert (Bn, Tn, Dn) == (2, 8192, 1024)
    x2d = x.reshape(Bn * Tn, Dn)

    shared = {
        "winT": np.ascontiguousarray(W_in.T),
        "waT": np.ascontiguousarray(W_a.T),
        "wbT": np.ascontiguousarray(W_b.T),
        "wvT": np.ascontiguousarray(W_v.T),
        "woT": np.ascontiguousarray(W_o.T),
        "w1T": np.ascontiguousarray(w1.T),
        "w2T": np.ascontiguousarray(w2.T),
        "bac": np.ascontiguousarray(b_a.reshape(DC, 128).T),
        "bbc": np.ascontiguousarray(b_b.reshape(DC, 128).T),
        "bvc": np.ascontiguousarray(b_v.reshape(DC, 128).T),
        "b1c": np.ascontiguousarray(b1.reshape(HC, 128).T),
        "b2r": np.ascontiguousarray(b2.reshape(1, D)),
        "w1n": np.ascontiguousarray(w1),
        "w2n": np.ascontiguousarray(w2),
    }
    import ml_dtypes
    shared["idt"] = np.eye(128, dtype=ml_dtypes.bfloat16)

    if "nc" not in _NC_CACHE:
        _NC_CACHE["nc"] = build_nc()
    nc = _NC_CACHE["nc"]

    in_maps = []
    for k in range(NCORE):
        grp = k // 4
        m = np.zeros((128, NCORE), dtype=np.float32)
        for jr in range(NCORE):
            if jr // 4 == grp and jr < k:
                m[:, jr] = 1.0
        im = dict(shared)
        im["mask"] = m
        im["x"] = np.ascontiguousarray(x2d[k * TCORE:(k + 1) * TCORE])
        in_maps.append(im)

    res = run_bass_kernel_spmd(nc, in_maps, core_ids=list(range(NCORE)))
    out = np.concatenate([np.asarray(res.results[k]["y"])
                          for k in range(NCORE)], axis=0)
    return out.reshape(Bn, Tn, Dn).astype(np.float32)


# version 28
# speedup vs baseline: 1.2210x; 1.2210x over previous
import numpy as np
from contextlib import ExitStack

import concourse.bass as bass
import concourse.mybir as mybir
from concourse.bass_utils import run_bass_kernel_spmd

F32 = mybir.dt.float32
BF16 = mybir.dt.bfloat16
AF = mybir.ActivationFunctionType
ALU = mybir.AluOpType

NCORE = 8
D = 1024
HID = 4096
TCORE = 2048
TT = 512
NTILE = TCORE // TT
GR = TT // 128
DC = D // 128
HC = HID // 128
NTG = TCORE // 128
EPS = 1e-5


class Builder:
    def __init__(self):
        self.nc = bass.Bass("TRN2", target_bir_lowering=False, debug=False)
        self.stacks = {}
        self.sems = {}
        self.cnt = {}
        self.hiw = {}
        self.bank_w = [None] * 8

    def stack(self, name):
        self.stacks[name] = ExitStack()
        return self.stacks[name]

    def sb(self, stack, name, shape, dt):
        return self.stacks[stack].enter_context(
            self.nc.sbuf_tensor(name, shape, dt))

    def mksem(self, *names):
        for n in names:
            self.sems[n] = self.stacks["perm"].enter_context(
                self.nc.semaphore(n))
            self.cnt[n] = 0

    def inc(self, instr, s, step=1):
        instr.then_inc(self.sems[s], step)
        self.cnt[s] += step
        return (s, self.cnt[s])

    def incD(self, instr, s):
        return self.inc(instr, s, 16)

    def tok(self, s):
        return (s, self.cnt[s])

    DEDUP = True

    def wt(self, eng, engname, *toks):
        for t in toks:
            if t is None:
                continue
            s, v = t
            if (not Builder.DEDUP) or v > self.hiw.get((engname, s), 0):
                eng.wait_ge(self.sems[s], v)
                self.hiw[(engname, s)] = v


def build_nc():
    B = Builder()
    nc = B.nc

    x_d = nc.dram_tensor("x", [TCORE, D], F32, kind="ExternalInput").ap()
    winT = nc.dram_tensor("winT", [D, D], F32, kind="ExternalInput").ap()
    waT = nc.dram_tensor("waT", [D, D], F32, kind="ExternalInput").ap()
    wbT = nc.dram_tensor("wbT", [D, D], F32, kind="ExternalInput").ap()
    wvT = nc.dram_tensor("wvT", [D, D], F32, kind="ExternalInput").ap()
    woT = nc.dram_tensor("woT", [D, D], F32, kind="ExternalInput").ap()
    w1T = nc.dram_tensor("w1T", [D, HID], F32, kind="ExternalInput").ap()
    w2T = nc.dram_tensor("w2T", [HID, D], F32, kind="ExternalInput").ap()
    w1n = nc.dram_tensor("w1n", [HID, D], F32, kind="ExternalInput").ap()
    w2n = nc.dram_tensor("w2n", [D, HID], F32, kind="ExternalInput").ap()
    bac_d = nc.dram_tensor("bac", [128, DC], F32, kind="ExternalInput").ap()
    bbc_d = nc.dram_tensor("bbc", [128, DC], F32, kind="ExternalInput").ap()
    bvc_d = nc.dram_tensor("bvc", [128, DC], F32, kind="ExternalInput").ap()
    b1c_d = nc.dram_tensor("b1c", [128, HC], F32, kind="ExternalInput").ap()
    b2r_d = nc.dram_tensor("b2r", [1, D], F32, kind="ExternalInput").ap()
    mask_d = nc.dram_tensor("mask", [128, NCORE], F32, kind="ExternalInput").ap()
    idt_d = nc.dram_tensor("idt", [128, 128], BF16, kind="ExternalInput").ap()
    y_d = nc.dram_tensor("y", [TCORE, D], F32, kind="ExternalOutput").ap()
    hsp = nc.dram_tensor("hsp", [NTILE, 128, DC * TT], F32, kind="Internal").ap()
    psp = nc.dram_tensor("psp", [NTILE, 128, DC * TT], F32, kind="Internal").ap()
    gsp = nc.dram_tensor("gsp", [NTG, 128, D], F32, kind="Internal").ap()
    osp = nc.dram_tensor("osp", [NTILE, 128, DC * TT], BF16, kind="Internal").ap()
    sw1d = nc.dram_tensor("sw1d", [HC, 128, DC, 128], BF16, kind="Internal").ap()
    sw2d = nc.dram_tensor("sw2d", [HC, 128, D], BF16, kind="Internal").ap()
    ccin = nc.dram_tensor("ccin", [128, 16], F32, kind="Internal")
    ccout = nc.dram_tensor("ccout", [NCORE * 128, 16], F32, kind="Internal")

    perm = B.stack("perm")
    B.mksem("qSet", "qWm", "qWpa", "qWpb", "qAa", "qAb", "qWs", "qRow",
            "qX0", "qX1", "qX2", "qX3",
            "qSp", "qC", "qWin", "qW1a", "qW1b", "qW1c", "qW1d", "qW1e", "qW1f",
            "qW2a", "qW2b", "qW2c", "qW2d", "qGs", "qR", "qR2", "qR3",
            "qOs", "qHr", "qY", "qP", "qYa", "qYb", "qYc", "qYd", "qR4",
            "cT", "cS", "cV", "cG", "cc")
    sems, cnt = B.sems, B.cnt
    inc, incD, tok, wt = B.inc, B.incD, B.tok, B.wt
    bank_w = B.bank_w

    idt = B.sb("perm", "idts", [128, 128], BF16)
    bac = B.sb("perm", "bacs", [128, DC], F32)
    bbc = B.sb("perm", "bbcs", [128, DC], F32)
    bvc = B.sb("perm", "bvcs", [128, DC], F32)
    b1c = B.sb("perm", "b1cs", [128, HC], F32)
    a1c = B.sb("perm", "a1cs", [128, HC], F32)
    a2c8 = B.sb("perm", "a2c8s", [128, DC], F32)
    a2cb = B.sb("perm", "a2cbs", [128, DC], BF16)
    a2row8 = B.sb("perm", "a2row8s", [8, 128], F32)
    b2r = B.sb("perm", "b2rs", [1, D], F32)
    b2p = B.sb("perm", "b2ps", [1, D], BF16)
    a2acc = B.sb("perm", "a2accs", [1, D], F32)
    a2rb = B.sb("perm", "a2rbs", [1, D], BF16)
    a2bc = B.sb("perm", "a2bcs", [128, D], BF16)
    ones_u = B.sb("perm", "ones_us", [1, 128], BF16)
    rtmp = B.sb("perm", "rtmps", [128, 2], F32)
    c15 = B.sb("perm", "c15s", [128, GR], F32)
    ch = B.sb("perm", "chs", [128, GR], F32)
    ceps = B.sb("perm", "cepss", [128, GR], F32)
    mask = B.sb("perm", "masks", [128, NCORE], F32)
    gat = B.sb("perm", "gats", [128, NCORE, 16], F32)
    Hin = B.sb("perm", "Hins", [128, DC], F32)
    t1c = B.sb("perm", "t1cs", [128, DC], F32)
    P_run = B.sb("perm", "P_runs", [128, DC], F32)
    S_run = B.sb("perm", "S_runs", [128, DC], F32)
    pub = B.sb("perm", "pubs", [128, 16], F32)
    mv = B.sb("perm", "mvs", [128, 2, GR, 2], F32)
    bn6 = B.sb("perm", "bn6s", [128, 12], F32)
    wvar = B.sb("perm", "wvars", [128, 2, GR], F32)
    rva = B.sb("perm", "rvas", [128, 2, GR], F32)
    uva = B.sb("perm", "uvas", [128, 2, GR], F32)

    ps = [perm.enter_context(nc.psum_tensor(f"ps{i}", [128, 512], F32))
          for i in range(7)]

    block = perm.enter_context(nc.Block())
    T, V, Sc, Sy, G = (block.tensor, block.vector, block.scalar, block.sync,
                       block.gpsimd)

    esX = B.stack("X")
    xnf = B.sb("X", "xnf", [128, DC, TCORE], BF16)
    esA = B.stack("A")
    win_s = B.sb("A", "win_s", [128, DC, D], BF16)
    wa_s = B.sb("A", "wa_s", [128, DC, D], BF16)
    wb_s = B.sb("A", "wb_s", [128, DC, D], BF16)
    wvv_s = B.sb("A", "wvv_s", [128, DC, D], BF16)
    xfull = B.sb("A", "xfull", [128, GR, D], F32)
    xntm = B.sb("A", "xntm", [128, 2, D], BF16)
    xpf = B.sb("A", "xpf", [128, DC, TT], BF16)
    atmp = B.sb("A", "atmp", [128, 2, TT], F32)
    btmp = B.sb("A", "btmp", [128, 2, TT], F32)
    vtm2 = B.sb("A", "vtm2", [128, 2, TT], F32)
    hloc = B.sb("A", "hloc", [128, 4, TT], F32)
    ploc = B.sb("A", "ploc", [128, 4, TT], F32)
    pst = B.stacks["A"].enter_context(
        nc.psum_tensor("pst", [128, 1024], BF16))
    wfpb = B.sb("A", "wfpb", [128, 2, 2048], F32)
    aab = B.sb("A", "aab", [128, 2, 1024], F32)
    signb = B.sb("A", "signb", [128, 2, 2048], BF16)

    def setup_sy(sy):
        for dst, src in ((idt, idt_d), (bac, bac_d), (bbc, bbc_d),
                         (bvc, bvc_d), (b1c, b1c_d), (b2r, b2r_d),
                         (mask, mask_d)):
            i = sy.dma_start(out=dst[:], in_=src)
            incD(i, "qSet")
    Sy(setup_sy)
    setup_tok = tok("qSet")


    def setup_g(g):
        i = g.dma_start(out=win_s[:],
                        in_=winT.rearrange("(kc p) o -> p kc o", p=128))
        incD(i, "qWin")
        for t_, val in ((ones_u, 1.0), (S_run, 0.0), (P_run, 1.0),
                        (Hin, 0.0), (a2c8, 0.0), (c15, 1.5), (ch, 0.5),
                        (ceps, EPS)):
            i = g.memset(t_[:], val)
            inc(i, "cG")
        for dst, src in ((wa_s, waT), (wb_s, wbT), (wvv_s, wvT)):
            i = g.dma_start(out=dst[:],
                            in_=src.rearrange("(kc p) o -> p kc o", p=128))
            incD(i, "qWm")
    G(setup_g)
    mixw_tok = tok("qWin")
    abvw_tok = tok("qWm")
    memset_tok = tok("cG")

    st = {
        "signb_free": None, "sign_prev": None,
        "a1_done": None, "prep_done": None, "a2bc_tok": None, "b2p_tok": None,
        "wfp_free": [None, None], "aab_free": [None, None],
        "a1_prev": None, "a2_prev": None, "rt_free": [None, None],
    }

    def sign_w1_load(d, h, slot):
        def g_load(g):
            wt(g, "G", st["wfp_free"][slot])
            i = g.dma_start(out=wfpb[:, slot, :],
                            in_=w1T[d * 128:(d + 1) * 128,
                                    h * 2048:(h + 1) * 2048])
            incD(i, "qWpa" if slot == 0 else "qWpb")
        G(g_load)
        return tok("qWpa" if slot == 0 else "qWpb")

    def sign_w2_load(m, slot):
        def g_load(g):
            wt(g, "G", st["wfp_free"][slot])
            i = g.dma_start(
                out=wfpb[:, slot, :].rearrange("p (c d) -> p c d", c=2),
                in_=w2T[m * 256:(m + 1) * 256, :].rearrange(
                    "(c p) d -> p c d", p=128))
            incD(i, "qWpa" if slot == 0 else "qWpb")
        G(g_load)
        return tok("qWpa" if slot == 0 else "qWpb")

    def sign_compute_pair(items):
        lds = [it[4] for it in items]

        def s_sign(s):
            wt(s, "S", st["signb_free"], st["sign_prev"], *lds)
            for kind, a_, h, slot, ld in items:
                i = s.activation(out=signb[:, slot, :], in_=wfpb[:, slot, :],
                                 func=AF.Sign)
                inc(i, "cS")
        Sc(s_sign)
        sg_t = tok("cS")
        st["sign_prev"] = sg_t
        for kind, a_, h, slot, ld in items:
            st["wfp_free"][slot] = sg_t

        def g_store(g):
            wt(g, "G", sg_t)
            for kind, a_, h, slot, ld in items:
                if kind == "w1":
                    ci0 = h * 16
                    i = g.dma_start(
                        out=sw1d[ci0:ci0 + 16, :, a_, :].rearrange(
                            "ci p c -> p ci c"),
                        in_=signb[:, slot, :].rearrange(
                            "p (ci c) -> p ci c", ci=16))
                else:
                    i = g.dma_start(
                        out=sw2d[2 * a_:2 * a_ + 2, :, :].rearrange(
                            "ci p d -> p ci d"),
                        in_=signb[:, slot, :].rearrange(
                            "p (ci d) -> p ci d", ci=2))
                incD(i, "qWs")
        G(g_store)
        st["signb_free"] = tok("qWs")

    def alpha_load(kind, k, slot):
        def sy_load(sy):
            wt(sy, "Y", st["aab_free"][slot])
            if kind == "a1":
                i = sy.dma_start(out=aab[:, slot, :],
                                 in_=w1n[k * 128:(k + 1) * 128, :])
            else:
                dc, q = k // 4, k % 4
                i = sy.dma_start(
                    out=aab[:, slot, :],
                    in_=w2n[dc * 128:(dc + 1) * 128,
                            q * 1024:(q + 1) * 1024])
            incD(i, "qAa" if slot == 0 else "qAb")
        Sy(sy_load)
        return tok("qAa" if slot == 0 else "qAb")

    def alpha_compute(kind, k, slot, ld):
        if kind == "a1":
            def v_red(v):
                wt(v, "V", ld, st["a1_prev"])
                i = v.reduce_sum(out=a1c[:, k:k + 1],
                                 in_=aab[:, slot, :],
                                 axis=mybir.AxisListType.X,
                                 apply_absolute_value=True)
                inc(i, "cV")
            V(v_red)
            st["a1_prev"] = tok("cV")
            st["aab_free"][slot] = tok("cV")
            if k == HC - 1:
                def v_sc(v):
                    wt(v, "V", st["a1_prev"])
                    i = v.tensor_scalar(out=a1c[:], in0=a1c[:],
                                        scalar1=1.0 / D, scalar2=None,
                                        op0=ALU.mult)
                    inc(i, "cV")
                V(v_sc)
                st["a1_done"] = tok("cV")
        else:
            dc = k // 4
            rs = k % 2

            def v_red(v, dc=dc, rs=rs):
                wt(v, "V", ld, st["rt_free"][rs])
                i = v.reduce_sum(out=rtmp[:, rs:rs + 1], in_=aab[:, slot, :],
                                 axis=mybir.AxisListType.X,
                                 apply_absolute_value=True)
                inc(i, "cV")
                wt(v, "V", tok("cV"), st["a2_prev"])
                i = v.tensor_add(a2c8[:, dc:dc + 1], a2c8[:, dc:dc + 1],
                                 rtmp[:, rs:rs + 1])
                inc(i, "cV")
            V(v_red)
            st["a2_prev"] = tok("cV")
            st["rt_free"][rs] = tok("cV")
            st["aab_free"][slot] = tok("cV")

    def alpha2_finalize():
        def v_sc(v):
            wt(v, "V", st["a2_prev"])
            i = v.tensor_scalar(out=a2c8[:], in0=a2c8[:], scalar1=1.0 / HID,
                                scalar2=None, op0=ALU.mult)
            inc(i, "cV")
        V(v_sc)
        sc_t = tok("cV")

        def s_cast(s):
            wt(s, "S", sc_t)
            i = s.copy(out=a2cb[:], in_=a2c8[:])
            inc(i, "cS")
        Sc(s_cast)
        cb_t = tok("cS")

        def t_tr(t):
            wt(t, "T", cb_t, A["pst_free"])
            i = t.transpose(pst[0:8, 0:128], a2cb[:], idt[:])
            inc(i, "cT")
        T(t_tr)
        tr_t = tok("cT")

        def s_row(s):
            wt(s, "S", tr_t)
            i = s.copy(out=a2row8[:], in_=pst[0:8, 0:128])
            inc(i, "cS")
        Sc(s_row)
        A["pst_free"] = tok("cS")
        row_t = tok("cS")

        def sy_row(sy):
            wt(sy, "Y", row_t)
            for r in range(8):
                i = sy.dma_start(out=a2acc[0:1, r * 128:(r + 1) * 128],
                                 in_=a2row8[r:r + 1, :])
                incD(i, "qRow")
        Sy(sy_row)
        acc_t = tok("qRow")

        def s_rb(s):
            wt(s, "S", acc_t)
            i = s.copy(out=a2rb[:], in_=a2acc[:])
            inc(i, "cS")
        Sc(s_rb)
        rb_t = tok("cS")

        def v_b2p(v):
            wt(v, "V", rb_t)
            i = v.reciprocal(out=a2acc[:], in_=a2acc[:])
            inc(i, "cV")
            wt(v, "V", tok("cV"))
            i = v.tensor_mul(a2acc[:], a2acc[:], b2r[:])
            inc(i, "cV")
        V(v_b2p)
        bp_t = tok("cV")

        def s_b2p(s):
            wt(s, "S", bp_t)
            i = s.copy(out=b2p[:], in_=a2acc[:])
            inc(i, "cS")
        Sc(s_b2p)
        st["b2p_tok"] = tok("cS")

        for sg in range(2):
            def t_bc(t, sg=sg):
                wt(t, "T", tok("cS"), bank_w[6])
                i = t.matmul(ps[6][:, :], ones_u[:],
                             a2rb[0:1, sg * 512:(sg + 1) * 512],
                             start=True, stop=True)
                inc(i, "cT")
            T(t_bc)
            bc_t = tok("cT")

            def s_bc(s, sg=sg, bc_t=bc_t):
                wt(s, "S", bc_t)
                i = s.copy(out=a2bc[:, sg * 512:(sg + 1) * 512],
                           in_=ps[6][:, :])
                inc(i, "cS")
            Sc(s_bc)
            bank_w[6] = tok("cS")
        st["a2bc_tok"] = tok("cS")
        st["prep_done"] = tok("qWs")

    A = {
        "apply_done": [None, None], "trans_done": [None, None],
        "stat_done": [None, None], "stats_last": [None, None],
        "sgt_free": [None, None], "abv_free": [None, None],
        "vtmp_free": None, "spill": [None] * NTILE,
        "scan_last": None, "rot": 0, "spill_h": [None, None],
        "apply_prev": [None, None], "sgt_prev": [None, None],
        "apply_g": [None] * GR, "x_ld": [None] * GR,
        "xp_e": [None] * DC,
        "sqrt_done": [None, None],
        "recip_prev": [None, None], "rsqrt_tok": None,
        "abv_prev": [None, None], "xpf_free": None,
        "xnf_tok": [None] * NTILE, "pst_free": None,
    }

    def a_ln_load(j, g_):
        gi = j * GR + g_

        def sy_load(sy, g_=g_, gi=gi):
            wt(sy, "Y", A["apply_g"][g_])
            i = sy.dma_start(out=xfull[:, g_, :],
                             in_=x_d[gi * 128:(gi + 1) * 128, :])
            incD(i, "qX%d" % g_)
        Sy(sy_load)
        A["x_ld"][g_] = tok("qX%d" % g_)

    def a_ln_stats(j, g_):
        tp = j % 2
        ld = A["x_ld"][g_]

        def v_stats(v, g_=g_, tp=tp, ld=ld):
            wt(v, "V", ld, memset_tok, A["sqrt_done"][tp])
            wt(v, "V", tok("cV"))
            v.bn_stats(out=bn6[:, 0:6], in_=xfull[:, g_, 0:512])
            i = v.bn_stats(out=bn6[:, 6:12], in_=xfull[:, g_, 512:1024])
            inc(i, "cV")
            wt(v, "V", tok("cV"))
            i = v.bn_aggr(out=mv[:, tp, g_, :], in_=bn6[:])
            inc(i, "cV")
        V(v_stats)
        A["stats_last"][tp] = tok("cV")

    def a_ln_rsqrt(j):
        tp = j % 2
        stats_last = A["stats_last"][tp]

        def s_sqv(s, tp=tp, stats_last=stats_last):
            wt(s, "S", stats_last, memset_tok, A["recip_prev"][tp])
            i = s.activation(out=uva[:, tp, :], in_=mv[:, tp, :, 1],
                             func=AF.Sqrt, bias=ceps[:, 0:1])
            inc(i, "cS")
        Sc(s_sqv)
        A["sqrt_done"][tp] = tok("cS")

        def v_rec(v, tp=tp):
            wt(v, "V", A["sqrt_done"][tp], tok("cV"))
            i = v.reciprocal(rva[:, tp, :], uva[:, tp, :])
            inc(i, "cV")
        V(v_rec)
        A["rsqrt_tok"] = tok("cV")
        A["recip_prev"][tp] = A["rsqrt_tok"]

    def a_ln_apply(j):
        tp = j % 2
        last_ev = None
        for g_ in range(GR):
            sl = g_ % 2

            def v_apply(v, g_=g_, sl=sl, tp=tp):
                wt(v, "V", A["rsqrt_tok"], A["trans_done"][sl],
                   A["apply_prev"][sl])
                i = v.tensor_scalar(out=xntm[:, sl, :], in0=xfull[:, g_, :],
                                    scalar1=mv[:, tp, g_, 0:1],
                                    scalar2=rva[:, tp, g_:g_ + 1],
                                    op0=ALU.subtract, op1=ALU.mult)
                inc(i, "cV")
            V(v_apply)
            ap_t = tok("cV")
            A["apply_prev"][sl] = ap_t
            A["apply_done"][sl] = ap_t
            A["apply_g"][g_] = ap_t

            def t_trans(t, sl=sl, ap_t=ap_t):
                wt(t, "T", ap_t, A["pst_free"], setup_tok)
                i = None
                for kc in range(DC):
                    i = t.transpose(
                        pst[:, kc * 128:(kc + 1) * 128],
                        xntm[:, sl, kc * 128:(kc + 1) * 128], idt[:])
                inc(i, "cT")
            T(t_trans)
            tr_t = tok("cT")

            def v_evac(v, j=j, g_=g_, tr_t=tr_t):
                wt(v, "V", tr_t)
                off = j * TT + g_ * 128
                i = v.tensor_copy(xnf[:, :, off:off + 128],
                                  pst[:].rearrange("p (k c) -> p k c", k=DC))
                inc(i, "cV")
            V(v_evac)
            A["pst_free"] = tok("cV")
            A["trans_done"][sl] = tok("cT")
            last_ev = tok("cV")
        A["xnf_tok"][j] = last_ev

    def a_mm(j, preps=(), lnj=None):
        preps = list(preps)

        def run_prep():
            if preps:
                preps.pop(0)()

        xnf_tok = A["xnf_tok"][j]
        for ep in range(DC // 2):
            mm_ts = []
            for e in (2 * ep, 2 * ep + 1):
                bank = e % 7

                def t_xp(t, e=e, bank=bank, j=j, xnf_tok=xnf_tok):
                    wt(t, "T", bank_w[bank], xnf_tok, mixw_tok)
                    i = None
                    for kc in range(DC):
                        i = t.matmul(ps[bank][:],
                                     win_s[:, kc, e * 128:(e + 1) * 128],
                                     xnf[:, kc, j * TT:(j + 1) * TT],
                                     start=(kc == 0), stop=(kc == DC - 1))
                    inc(i, "cT")
                T(t_xp)
                mm_ts.append(tok("cT"))

            def s_silu(s, ep=ep, mm_ts=mm_ts):
                wt(s, "S", mm_ts[0], A["xpf_free"], A["sgt_prev"][1])
                for ix, e in enumerate((2 * ep, 2 * ep + 1)):
                    if ix == 1:
                        wt(s, "S", mm_ts[1])
                    i = s.activation(out=xpf[:, e, :],
                                     in_=ps[e % 7][:], func=AF.Silu)
                    inc(i, "cS")
                    bank_w[e % 7] = tok("cS")
                    A["xp_e"][e] = tok("cS")
            Sc(s_silu)
            A["sgt_prev"][1] = tok("cS")
            run_prep()

        if lnj is not None:
            for g_ in range(GR):
                a_ln_load(lnj, g_)
            for g_ in range(GR):
                a_ln_stats(lnj, g_)

        for fp in range(DC // 2):
            fbanks = []
            mmts = []
            for f in (2 * fp, 2 * fp + 1):
                ba = A["rot"] % 7
                bb_ = (A["rot"] + 1) % 7
                bv_ = (A["rot"] + 2) % 7
                A["rot"] += 3
                fbanks.append((ba, bb_, bv_))

                def t_abv(t, f=f, ba=ba, bb_=bb_, bv_=bv_):
                    wt(t, "T", abvw_tok)
                    for wsb, bank in ((wa_s, ba), (wb_s, bb_), (wvv_s, bv_)):
                        wt(t, "T", bank_w[bank])
                        i = None
                        for e in range(DC):
                            wt(t, "T", A["xp_e"][e])
                            i = t.matmul(ps[bank][:],
                                         wsb[:, e, f * 128:(f + 1) * 128],
                                         xpf[:, e, :],
                                         start=(e == 0), stop=(e == DC - 1))
                        inc(i, "cT")
                T(t_abv)
                mmts.append(tok("cT"))

            def s_abv(s, fp=fp, fbanks=fbanks, mmts=mmts):
                abv2 = {}
                for sl in range(2):
                    f = 2 * fp + sl
                    ba, bb_, bv_ = fbanks[sl]
                    wt(s, "S", mmts[sl], A["abv_free"][sl],
                       A["abv_prev"][sl])
                    i = s.activation(out=atmp[:, sl, :], in_=ps[ba][:],
                                     func=AF.Sigmoid, bias=bac[:, f:f + 1])
                    inc(i, "cS")
                    bank_w[ba] = tok("cS")
                    i = s.activation(out=btmp[:, sl, :], in_=ps[bb_][:],
                                     func=AF.Sigmoid, bias=bbc[:, f:f + 1])
                    inc(i, "cS")
                    bank_w[bb_] = tok("cS")
                    i = s.activation(out=vtm2[:, sl, :], in_=ps[bv_][:],
                                     func=AF.Tanh, bias=bvc[:, f:f + 1])
                    inc(i, "cS")
                    bank_w[bv_] = tok("cS")
                    abv2[sl] = tok("cS")
                    A["abv_prev"][sl] = abv2[sl]
                A["abv2"] = abv2
            Sc(s_abv)
            abv2 = A["abv2"]

            for f in (2 * fp, 2 * fp + 1):
                slot = f % 2
                abv_t = abv2[slot]

                def v_scan(v, f=f, slot=slot, abv_t=abv_t):
                    wt(v, "V", abv_t, A["spill_h"][1 - f // 4])
                    i = v.tensor_mul(btmp[:, slot, :], btmp[:, slot, :],
                                     vtm2[:, slot, :])
                    inc(i, "cV")
                    wt(v, "V", tok("cV"))
                    i = v.tensor_tensor_scan(out=hloc[:, f % 4, :],
                                             data0=atmp[:, slot, :],
                                             data1=btmp[:, slot, :],
                                             initial=S_run[:, f:f + 1],
                                             op0=ALU.mult, op1=ALU.add)
                    inc(i, "cV")
                    i = v.tensor_tensor_scan(out=ploc[:, f % 4, :],
                                             data0=atmp[:, slot, :],
                                             data1=atmp[:, slot, :],
                                             initial=P_run[:, f:f + 1],
                                             op0=ALU.mult, op1=ALU.min)
                    inc(i, "cV")
                V(v_scan)
                A["abv_free"][slot] = tok("cV")
                if f % 4 == 3:
                    hh = f // 4

                    def v_sth(v, hh=hh):
                        wt(v, "V", tok("cV"))
                        i = v.tensor_copy(S_run[:, hh * 4:hh * 4 + 4],
                                          hloc[:, :, TT - 1])
                        inc(i, "cV")
                        i = v.tensor_copy(P_run[:, hh * 4:hh * 4 + 4],
                                          ploc[:, :, TT - 1])
                        inc(i, "cV")
                    V(v_sth)
                    sc = tok("cV")
                    if hh == 1:
                        A["scan_last"] = sc

                    def g_spill(g, j=j, hh=hh, sc=sc):
                        wt(g, "G", sc)
                        i = g.dma_start(
                            out=hsp[j, :, hh * 2048:(hh + 1) * 2048],
                            in_=hloc[:].rearrange("p a b -> p (a b)"))
                        incD(i, "qSp")
                        i = g.dma_start(
                            out=psp[j, :, hh * 2048:(hh + 1) * 2048],
                            in_=ploc[:].rearrange("p a b -> p (a b)"))
                        incD(i, "qSp")
                    G(g_spill)
                    A["spill_h"][hh] = tok("qSp")
                    if hh == 1:
                        A["spill"][j] = A["spill_h"][hh]
            if lnj is not None and fp == 1:
                a_ln_rsqrt(lnj)
            run_prep()
            run_prep()

        A["xpf_free"] = tok("cT")

        if lnj is not None:
            a_ln_apply(lnj)

    sign_slices = [("w1", d, h) for d in range(DC) for h in range(2)] + \
                  [("w2", m, 0) for m in range(16)]
    alpha_slices = [("a1", k) for k in range(HC)] + \
                   [("a2", k) for k in range(32)]
    pstate = {"sp": [], "ap": []}

    def prep_point(k):
        sp, ap = pstate["sp"], pstate["ap"]
        if len(sp) >= 2 and (k % 2 == 0 or k >= 32):
            items = [sp.pop(0), sp.pop(0)]
            sign_compute_pair(items)
        elif k >= 32 and sp:
            sign_compute_pair([sp.pop(0)])
        if k < 32:
            kind, a_, h = sign_slices[k]
            slot = k % 2
            ld = (sign_w1_load(a_, h, slot) if kind == "w1"
                  else sign_w2_load(a_, slot))
            sp.append((kind, a_, h, slot, ld))
        for u in range(2):
            if len(ap) >= 2 or (k >= 32 and ap):
                kind, k_, slot, ld = ap.pop(0)
                alpha_compute(kind, k_, slot, ld)
            ku = 2 * k + u
            if ku < 64:
                kind, k_ = alpha_slices[ku]
                slot = ku % 2
                ld = alpha_load(kind, k_, slot)
                ap.append((kind, k_, slot, ld))

    for g_ in range(GR):
        a_ln_load(0, g_)
    for g_ in range(GR):
        a_ln_stats(0, g_)
    a_ln_rsqrt(0)
    a_ln_apply(0)
    pp_off = [0, 11, 22, 32, 32]
    for j in range(NTILE):
        a_mm(j, [(lambda k: (lambda: prep_point(k)))(pp_off[j] + q)
                 for q in range(pp_off[j + 1] - pp_off[j])],
             lnj=j + 1 if j + 1 < NTILE else None)
    k = 32
    while pstate["sp"] or pstate["ap"]:
        prep_point(k)
        k += 1
    alpha2_finalize()

    def v_pub(v):
        wt(v, "V", A["scan_last"])
        i = v.tensor_copy(pub[:, 0:8], P_run[:])
        inc(i, "cV")
        i = v.tensor_copy(pub[:, 8:16], S_run[:])
        inc(i, "cV")
    V(v_pub)
    pub_tok = tok("cV")

    def sy_pub(sy):
        wt(sy, "Y", pub_tok)
        i = sy.dma_start(out=ccin.ap(), in_=pub[:])
        incD(i, "qC")
    Sy(sy_pub)
    ccin_tok = tok("qC")

    def g_cc(g):
        wt(g, "G", ccin_tok)
        i = g.collective_compute(
            "AllGather", ALU.bypass,
            replica_groups=[list(range(NCORE))],
            ins=[ccin.ap()], outs=[ccout.ap()])
        inc(i, "cc")
    G(g_cc)
    cc_tok = tok("cc")

    def sy_gat(sy):
        wt(sy, "Y", cc_tok)
        i = sy.dma_start(out=gat[:],
                         in_=ccout.ap().rearrange("(r p) c -> p r c", p=128))
        incD(i, "qC")
    Sy(sy_gat)
    gat_tok = tok("qC")

    def v_compose(v):
        wt(v, "V", gat_tok, memset_tok, setup_tok)
        for jr in range(NCORE):
            wt(v, "V", tok("cV"))
            i = v.tensor_mul(t1c[:], gat[:, jr, 0:8], Hin[:])
            inc(i, "cV")
            wt(v, "V", tok("cV"))
            i = v.tensor_add(t1c[:], t1c[:], gat[:, jr, 8:16])
            inc(i, "cV")
            wt(v, "V", tok("cV"))
            i = v.tensor_sub(t1c[:], t1c[:], Hin[:])
            inc(i, "cV")
            wt(v, "V", tok("cV"))
            i = v.scalar_tensor_tensor(out=Hin[:], in0=t1c[:],
                                       scalar=mask[:, jr:jr + 1], in1=Hin[:],
                                       op0=ALU.mult, op1=ALU.add)
            inc(i, "cV")
    V(v_compose)
    hin_tok = tok("cV")

    def setup_pro(s):
        i = s.dma_start(out=y_d[:], in_=x_d[:])
        incD(i, "qP")
    Sc(setup_pro)
    pro_tok = tok("qP")

    aend = (A["spill"][NTILE - 1], A["scan_last"], tok("cT"), tok("cS"),
            st["prep_done"])
    esA.close()
    esH = B.stack("H")
    ps.append(esH.enter_context(nc.psum_tensor("ps7", [128, 512], F32)))
    h1 = B.sb("H", "h1", [128, HC, TCORE], BF16)
    esC = B.stack("C")
    hlo = B.sb("C", "hlo", [128, DC, 256], F32)
    plo = B.sb("C", "plo", [128, DC, 256], F32)
    hht = B.sb("C", "hht", [128, 256], F32)
    otst = B.sb("C", "otst", [128, DC, 256], BF16)

    C = {"hp_free": None, "ot_free": None, "th_prev": None, "hh_free": None}

    def corr_unit(u):
        j, hh = u // 2, u % 2
        c0 = hh * 256

        def sy_rel(sy, j=j, c0=c0, u=u):
            wt(sy, "Y", C["hp_free"], tok("qSp"))
            if u == 0:
                wt(sy, "Y", st["prep_done"], st["sign_prev"],
                   st["a1_prev"], st["a2_prev"], A["scan_last"])
            i = sy.dma_start(
                out=hlo[:],
                in_=hsp[j, :, :].rearrange("p (f t) -> p f t", f=DC)[
                    :, :, c0:c0 + 256])
            incD(i, "qHr")
            i = sy.dma_start(
                out=plo[:],
                in_=psp[j, :, :].rearrange("p (f t) -> p f t", f=DC)[
                    :, :, c0:c0 + 256])
            incD(i, "qHr")
        Sy(sy_rel)
        hp_t = tok("qHr")

        def v_corr(v, hp_t=hp_t):
            wt(v, "V", hp_t, hin_tok, C["hh_free"])
            for f in range(DC):
                i = v.scalar_tensor_tensor(out=hlo[:, f, :], in0=plo[:, f, :],
                                           scalar=Hin[:, f:f + 1],
                                           in1=hlo[:, f, :],
                                           op0=ALU.mult, op1=ALU.add)
                inc(i, "cV")
        V(v_corr)
        corr_t = tok("cV")
        C["hp_free"] = corr_t

        def s_th(s, corr_t=corr_t):
            wt(s, "S", corr_t, C["th_prev"], C["ot_free"])
            for f in range(DC):
                i = s.activation(out=otst[:, f, :], in_=hlo[:, f, :],
                                 func=AF.Tanh, scale=0.5)
                inc(i, "cS")
        Sc(s_th)
        th_t = tok("cS")
        C["th_prev"] = th_t

        def v_o(v, th_t=th_t):
            wt(v, "V", th_t)
            for f in range(DC):
                wt(v, "V", tok("cV"))
                i = v.tensor_mul(hht[:], hlo[:, f, :], otst[:, f, :])
                inc(i, "cV")
                wt(v, "V", tok("cV"))
                i = v.tensor_add(hht[:], hht[:], hlo[:, f, :])
                inc(i, "cV")
                wt(v, "V", tok("cV"))
                i = v.scalar_tensor_tensor(out=otst[:, f, :],
                                           in0=hht[:], scalar=0.5,
                                           in1=hlo[:, f, :],
                                           op0=ALU.mult, op1=ALU.mult)
                inc(i, "cV")
        V(v_o)
        C["hh_free"] = tok("cV")
        o_done = tok("cV")
        o_done = tok("cV")

        def sy_osp(sy, j=j, c0=c0, o_done=o_done):
            wt(sy, "Y", o_done)
            i = sy.dma_start(
                out=osp[j, :, :].rearrange("p (f t) -> p f t", f=DC)[
                    :, :, c0:c0 + 256],
                in_=otst[:])
            incD(i, "qOs")
        Sy(sy_osp)
        C["ot_free"] = tok("qOs")
    esS1 = B.stack("S1")
    sw1r = B.sb("S1", "sw1r", [128, 5, DC, 128], BF16)

    sw1_free = [None] * 5
    W1Q = ["qW1a", "qW1b", "qW1c", "qW1d", "qW1e"]
    gelu_toks = []
    for ci in range(HC):
        if ci % 4 == 0:
            corr_unit(ci // 4)
        slot = ci % 5

        def g_ld(g, ci=ci, slot=slot):
            if ci < 5:
                wt(g, "G", *aend)
            wt(g, "G", sw1_free[slot], st["prep_done"])
            i = g.dma_start(out=sw1r[:, slot, :, :], in_=sw1d[ci, :, :, :])
            incD(i, W1Q[slot])
        G(g_ld)
        ld = tok(W1Q[slot])

        def t_h1(t, ci=ci, slot=slot, ld=ld):
            wt(t, "T", ld, st["a1_done"])
            for jt in range(NTILE):
                wt(t, "T", bank_w[(ci % 2) * 4 + jt])
            i = None
            for d in range(DC):
                for jt in range(NTILE):
                    i = t.matmul(ps[(ci % 2) * 4 + jt][:],
                                 sw1r[:, slot, d, :],
                                 xnf[:, d, jt * TT:(jt + 1) * TT],
                                 start=(d == 0), stop=(d == DC - 1))
                    if d == DC - 1:
                        inc(i, "cT")
        T(t_h1)
        mm_end = cnt["cT"]

        def s_gelu(s, ci=ci, mm_end=mm_end):
            for jt in range(NTILE):
                wt(s, "S", ("cT", mm_end - (NTILE - 1 - jt)))
                i = s.activation(out=h1[:, ci, jt * TT:(jt + 1) * TT],
                                 in_=ps[(ci % 2) * 4 + jt][:], func=AF.Gelu,
                                 scale=a1c[:, ci:ci + 1], bias=b1c[:, ci:ci + 1])
                inc(i, "cS")
                bank_w[(ci % 2) * 4 + jt] = tok("cS")
        Sc(s_gelu)
        sw1_free[slot] = tok("cS")
        gelu_toks.append(tok("cS"))
    h1_tok = gelu_toks[-1]

    b1a_end = tok("cT")
    esS1.close()
    esC.close()
    corr_done = (C["ot_free"], tok("cG"))
    esS2 = B.stack("S2")
    sw2r = B.sb("S2", "sw2r", [128, 4, D], BF16)
    gst = B.sb("S2", "gst", [128, GR, D], F32)

    sw2_free = [(b1a_end, *corr_done)] * 4
    sw2_free = [b1a_end] * 4
    cd0, cd1 = corr_done
    W2Q = ["qW2a", "qW2b", "qW2c", "qW2d"]
    gst_free = None

    gt2b = xnf.bitcast(F32).reshape([128, 2, GR, D])
    rel2_tok = [None] * NTILE
    g_done = [None, None]

    def b2_rel2(j, war):
        sl = j % 2
        q = "qR2" if sl == 0 else "qR4"

        def sy_rel2(sy, j=j, sl=sl, q=q, war=war):
            wt(sy, "Y", b1a_end, war)
            i = sy.dma_start(out=gt2b[:, sl, :, :],
                             in_=gsp[j * GR:(j + 1) * GR, :, :].rearrange(
                                 "t p c -> p t c"))
            incD(i, q)
        Sy(sy_rel2)
        rel2_tok[j] = tok(q)
    for tgg in range(4):
        def t_ones(t, tgg=tgg):
            wt(t, "T", st["b2p_tok"], h1_tok)
            for k in range(8):
                wt(t, "T", bank_w[k])
            for tg4 in range(4):
                for hf in range(2):
                    t.matmul(ps[tg4 * 2 + hf][:], ones_u[:],
                             b2p[0:1, hf * 512:(hf + 1) * 512],
                             start=True, stop=False)
        T(t_ones)

        for ci in range(HC):
            slot = ci % 4

            def g_ld(g, ci=ci, slot=slot):
                wt(g, "G", sw2_free[slot], cd0, cd1)
                i = g.dma_start(out=sw2r[:, slot, :], in_=sw2d[ci, :, :])
                incD(i, W2Q[slot])
            G(g_ld)
            ld = tok(W2Q[slot])

            def t_w2(t, tgg=tgg, ci=ci, slot=slot, ld=ld):
                wt(t, "T", ld)
                i = None
                for tg4 in range(4):
                    tg = tgg * 4 + tg4
                    for hf in range(2):
                        i = t.matmul(
                            ps[tg4 * 2 + hf][:],
                            h1[:, ci, tg * 128:(tg + 1) * 128],
                            sw2r[:, slot, hf * 512:(hf + 1) * 512],
                            start=False, stop=(ci == HC - 1))
                        if ci == HC - 1 or (tg4 == 3 and hf == 1):
                            inc(i, "cT")
            T(t_w2)
            sw2_free[slot] = tok("cT")
        w2_end = cnt["cT"]

        def v_gev(v, tgg=tgg, w2_end=w2_end, gf=gst_free):
            wt(v, "V", ("cT", w2_end), st["a2bc_tok"], gf)
            for tg4 in range(4):
                for hf in range(2):
                    i = v.tensor_mul(gst[:, tg4, hf * 512:(hf + 1) * 512],
                                     ps[tg4 * 2 + hf][:],
                                     a2bc[:, hf * 512:(hf + 1) * 512])
                    inc(i, "cV")
                    bank_w[tg4 * 2 + hf] = tok("cV")
        V(v_gev)
        gev_t = tok("cV")

        def sy_gsp(sy, tgg=tgg, gev_t=gev_t):
            wt(sy, "Y", gev_t)
            for tg4 in range(4):
                i = sy.dma_start(out=gsp[tgg * 4 + tg4, :, :],
                                 in_=gst[:, tg4, :])
                incD(i, "qGs")
        Sy(sy_gsp)
        gst_free = tok("qGs")
        if tgg < 2:
            b2_rel2(tgg, gst_free)

    b1b_end = tok("cT")
    esS2.close()
    esH.close()
    esB2 = B.stack("B2")
    wo_s = B.sb("B2", "wo_s", [128, DC, D], BF16)

    def g_wo(g):
        wt(g, "G", tok("cT"))
        i = g.dma_start(out=wo_s[:],
                        in_=woT.rearrange("(kc p) o -> p kc o", p=128))
        incD(i, "qWm")
    G(g_wo)
    wo_tok = tok("qWm")
    otb = B.sb("B2", "otb", [128, 2, DC, TT], BF16)
    gs2 = B.sb("B2", "gs2", [128, 2, 512], F32)
    sv2 = B.sb("B2", "sv2", [128, 4, 2, 512], F32)

    b2tok = {
        "ot_free": [b1b_end, b1b_end],
        "g_free": gst_free,
        "gs_free": [b1b_end, b1b_end], "sv_free": [None, None],
        "gs_prev": b1b_end, "acc_done": [None] * 4,
    }
    ot_tok = [None] * NTILE
    final_toks = []

    def b2_rel(j):
        sl = j % 2
        q = "qR" if sl == 0 else "qR3"

        def sy_rel(sy, j=j, sl=sl, q=q):
            wt(sy, "Y", b2tok["ot_free"][sl])
            if j == 1:
                wt(sy, "Y", ot_tok[0][1])
            if j < 2:
                wt(sy, "Y", b2tok["g_free"])
                wt(sy, "Y", tok("qOs"))
            if j == 0:
                i = sy.dma_start(out=otb[:, sl, 0:4, :].rearrange(
                                     "p a b -> p (a b)"),
                                 in_=osp[j, :, 0:2048])
                incD(i, "qR")
                i = sy.dma_start(out=otb[:, sl, 4:8, :].rearrange(
                                     "p a b -> p (a b)"),
                                 in_=osp[j, :, 2048:4096])
                incD(i, "qR3")
            else:
                i = sy.dma_start(out=otb[:, sl, :, :].rearrange(
                                     "p a b -> p (a b)"),
                                 in_=osp[j, :, :])
                incD(i, q)
        Sy(sy_rel)
        if j == 0:
            ot_tok[j] = (tok("qR"), tok("qR3"))
        else:
            ot_tok[j] = (tok(q), tok(q))

    def b2_tile(j):
        sl = j % 2
        rel_t = rel2_tok[j]

        for tg4 in range(GR):
            tg = j * GR + tg4
            mm_ts = {}
            for hf in range(2):
                bank = (tg4 % 2) * 2 + hf

                def t_s(t, j=j, sl=sl, tg4=tg4, hf=hf, bank=bank):
                    wt(t, "T", ot_tok[j][0], wo_tok, bank_w[bank])
                    i = None
                    for f in range(DC):
                        if f == 4:
                            wt(t, "T", ot_tok[j][1])
                        i = t.matmul(ps[bank][:],
                                     otb[:, sl, f, tg4 * 128:(tg4 + 1) * 128],
                                     wo_s[:, f, hf * 512:(hf + 1) * 512],
                                     start=(f == 0), stop=(f == DC - 1))
                    inc(i, "cT")
                T(t_s)
                mm_ts[hf] = tok("cT")

            def s_gs(s, sl=sl, tg4=tg4, rel_t=rel_t):
                wt(s, "S", rel_t, b2tok["gs_free"][0], b2tok["gs_free"][1],
                   b2tok["gs_prev"])
                for hf in range(2):
                    i = s.activation(out=gs2[:, hf, :],
                                     in_=gt2b[:, sl, tg4,
                                              hf * 512:(hf + 1) * 512],
                                     func=AF.Sigmoid)
                    inc(i, "cS")
            Sc(s_gs)
            gs_t = tok("cS")
            b2tok["gs_prev"] = gs_t

            tpar = tg4 % 4
            for hf in range(2):
                bank = (tg4 % 2) * 2 + hf

                def v_fin(v, sl=sl, tg4=tg4, tpar=tpar, hf=hf, bank=bank,
                          s_mm=mm_ts[hf], gs_t=gs_t):
                    wt(v, "V", gs_t, s_mm, b2tok["acc_done"][tpar])
                    i = v.tensor_mul(gt2b[:, sl, tg4,
                                          hf * 512:(hf + 1) * 512],
                                     gt2b[:, sl, tg4,
                                          hf * 512:(hf + 1) * 512],
                                     gs2[:, hf, :])
                    inc(i, "cV")
                    b2tok["gs_free"][hf] = tok("cV")
                    wt(v, "V", tok("cV"))
                    i = v.tensor_mul(sv2[:, tpar, hf, :], ps[bank][:],
                                     gt2b[:, sl, tg4,
                                          hf * 512:(hf + 1) * 512])
                    inc(i, "cV")
                    bank_w[bank] = tok("cV")
                V(v_fin)

            fin_t = tok("cV")
            qacc = ("qYa", "qYb", "qYc", "qYd")[tpar]

            def g_acc(g, tg=tg, tpar=tpar, fin_t=fin_t, qacc=qacc):
                wt(g, "G", fin_t, pro_tok)
                i = g.dma_start(out=y_d[tg * 128:(tg + 1) * 128, :],
                                in_=sv2[:, tpar, :, :].rearrange(
                                    "p a b -> p (a b)"),
                                accum_op=ALU.add)
                incD(i, qacc)
            G(g_acc)
            b2tok["acc_done"][tpar] = tok(qacc)
            final_toks.append(tok(qacc))
        b2tok["ot_free"][sl] = tok("cT")
        b2tok["g_free"] = tok("cV")
        g_done[sl] = tok("cV")

    b2_rel(0)
    for j in range(NTILE):
        if j + 1 < NTILE:
            b2_rel(j + 1)
        b2_tile(j)
        if j + 2 < NTILE:
            b2_rel2(j + 2, g_done[j % 2])

    def sy_end(sy):
        wt(sy, "Y", pro_tok, *final_toks[-4:])
    Sy(sy_end)

    esB2.close()
    esX.close()
    B.stacks["perm"].close()
    nc._dbg_cnt = dict(cnt)
    return nc


_NC_CACHE = {}


def kernel(**inputs):
    x = np.ascontiguousarray(np.asarray(inputs["L0_attn_x"], dtype=np.float32))
    W_in = np.asarray(inputs["L0_attn_W_in"], dtype=np.float32)
    W_a = np.asarray(inputs["L0_attn_W_alpha"], dtype=np.float32)
    b_a = np.asarray(inputs["L0_attn_b_alpha"], dtype=np.float32)
    W_b = np.asarray(inputs["L0_attn_W_beta"], dtype=np.float32)
    b_b = np.asarray(inputs["L0_attn_b_beta"], dtype=np.float32)
    W_v = np.asarray(inputs["L0_attn_W_v"], dtype=np.float32)
    b_v = np.asarray(inputs["L0_attn_b_v"], dtype=np.float32)
    W_o = np.asarray(inputs["L0_attn_W_out"], dtype=np.float32)
    w1 = np.asarray(inputs["L0_ffn_w1"], dtype=np.float32)
    b1 = np.asarray(inputs["L0_ffn_b1"], dtype=np.float32)
    w2 = np.asarray(inputs["L0_ffn_w2"], dtype=np.float32)
    b2 = np.asarray(inputs["L0_ffn_b2"], dtype=np.float32)

    Bn, Tn, Dn = x.shape
    assert (Bn, Tn, Dn) == (2, 8192, 1024)
    x2d = x.reshape(Bn * Tn, Dn)

    shared = {
        "winT": np.ascontiguousarray(W_in.T),
        "waT": np.ascontiguousarray(W_a.T),
        "wbT": np.ascontiguousarray(W_b.T),
        "wvT": np.ascontiguousarray(W_v.T),
        "woT": np.ascontiguousarray(W_o.T),
        "w1T": np.ascontiguousarray(w1.T),
        "w2T": np.ascontiguousarray(w2.T),
        "bac": np.ascontiguousarray(b_a.reshape(DC, 128).T),
        "bbc": np.ascontiguousarray(b_b.reshape(DC, 128).T),
        "bvc": np.ascontiguousarray(b_v.reshape(DC, 128).T),
        "b1c": np.ascontiguousarray(b1.reshape(HC, 128).T),
        "b2r": np.ascontiguousarray(b2.reshape(1, D)),
        "w1n": np.ascontiguousarray(w1),
        "w2n": np.ascontiguousarray(w2),
    }
    import ml_dtypes
    shared["idt"] = np.eye(128, dtype=ml_dtypes.bfloat16)

    if "nc" not in _NC_CACHE:
        _NC_CACHE["nc"] = build_nc()
    nc = _NC_CACHE["nc"]

    in_maps = []
    for k in range(NCORE):
        grp = k // 4
        m = np.zeros((128, NCORE), dtype=np.float32)
        for jr in range(NCORE):
            if jr // 4 == grp and jr < k:
                m[:, jr] = 1.0
        im = dict(shared)
        im["mask"] = m
        im["x"] = np.ascontiguousarray(x2d[k * TCORE:(k + 1) * TCORE])
        in_maps.append(im)

    res = run_bass_kernel_spmd(nc, in_maps, core_ids=list(range(NCORE)))
    out = np.concatenate([np.asarray(res.results[k]["y"])
                          for k in range(NCORE)], axis=0)
    return out.reshape(Bn, Tn, Dn).astype(np.float32)


# version 29
# speedup vs baseline: 1.2709x; 1.0408x over previous
import numpy as np
from contextlib import ExitStack

import concourse.bass as bass
import concourse.mybir as mybir
from concourse.bass_utils import run_bass_kernel_spmd

F32 = mybir.dt.float32
F16 = mybir.dt.float16
BF16 = mybir.dt.bfloat16
AF = mybir.ActivationFunctionType
ALU = mybir.AluOpType

NCORE = 8
D = 1024
HID = 4096
TCORE = 2048
TT = 512
NTILE = TCORE // TT
GR = TT // 128
DC = D // 128
HC = HID // 128
NTG = TCORE // 128
EPS = 1e-5


class Builder:
    def __init__(self):
        self.nc = bass.Bass("TRN2", target_bir_lowering=False, debug=False)
        self.stacks = {}
        self.sems = {}
        self.cnt = {}
        self.hiw = {}
        self.bank_w = [None] * 8

    def stack(self, name):
        self.stacks[name] = ExitStack()
        return self.stacks[name]

    def sb(self, stack, name, shape, dt):
        return self.stacks[stack].enter_context(
            self.nc.sbuf_tensor(name, shape, dt))

    def mksem(self, *names):
        for n in names:
            self.sems[n] = self.stacks["perm"].enter_context(
                self.nc.semaphore(n))
            self.cnt[n] = 0

    def inc(self, instr, s, step=1):
        instr.then_inc(self.sems[s], step)
        self.cnt[s] += step
        return (s, self.cnt[s])

    def incD(self, instr, s):
        return self.inc(instr, s, 16)

    def tok(self, s):
        return (s, self.cnt[s])

    DEDUP = True

    def wt(self, eng, engname, *toks):
        for t in toks:
            if t is None:
                continue
            s, v = t
            if (not Builder.DEDUP) or v > self.hiw.get((engname, s), 0):
                eng.wait_ge(self.sems[s], v)
                self.hiw[(engname, s)] = v


def build_nc():
    B = Builder()
    nc = B.nc

    x_d = nc.dram_tensor("x", [TCORE, D], F32, kind="ExternalInput").ap()
    winT = nc.dram_tensor("winT", [D, D], F32, kind="ExternalInput").ap()
    waT = nc.dram_tensor("waT", [D, D], F32, kind="ExternalInput").ap()
    wbT = nc.dram_tensor("wbT", [D, D], F32, kind="ExternalInput").ap()
    wvT = nc.dram_tensor("wvT", [D, D], F32, kind="ExternalInput").ap()
    woT = nc.dram_tensor("woT", [D, D], F32, kind="ExternalInput").ap()
    w1T = nc.dram_tensor("w1T", [D, HID], F32, kind="ExternalInput").ap()
    w2T = nc.dram_tensor("w2T", [HID, D], F32, kind="ExternalInput").ap()
    w1n = nc.dram_tensor("w1n", [HID, D], F32, kind="ExternalInput").ap()
    w2n = nc.dram_tensor("w2n", [D, HID], F32, kind="ExternalInput").ap()
    bac_d = nc.dram_tensor("bac", [128, DC], F32, kind="ExternalInput").ap()
    bbc_d = nc.dram_tensor("bbc", [128, DC], F32, kind="ExternalInput").ap()
    bvc_d = nc.dram_tensor("bvc", [128, DC], F32, kind="ExternalInput").ap()
    b1c_d = nc.dram_tensor("b1c", [128, HC], F32, kind="ExternalInput").ap()
    b2r_d = nc.dram_tensor("b2r", [1, D], F32, kind="ExternalInput").ap()
    mask_d = nc.dram_tensor("mask", [128, NCORE], F32, kind="ExternalInput").ap()
    idt_d = nc.dram_tensor("idt", [128, 128], BF16, kind="ExternalInput").ap()
    y_d = nc.dram_tensor("y", [TCORE, D], F32, kind="ExternalOutput").ap()
    hsp = nc.dram_tensor("hsp", [NTILE, 128, DC * TT], F16, kind="Internal").ap()
    psp = nc.dram_tensor("psp", [NTILE, 128, DC * TT], F16, kind="Internal").ap()
    gsp = nc.dram_tensor("gsp", [NTG, 128, D], F16, kind="Internal").ap()
    osp = nc.dram_tensor("osp", [NTILE, 128, DC * TT], BF16, kind="Internal").ap()
    sw1d = nc.dram_tensor("sw1d", [HC, 128, DC, 128], BF16, kind="Internal").ap()
    sw2d = nc.dram_tensor("sw2d", [HC, 128, D], BF16, kind="Internal").ap()
    ccin = nc.dram_tensor("ccin", [128, 16], F32, kind="Internal")
    ccout = nc.dram_tensor("ccout", [NCORE * 128, 16], F32, kind="Internal")

    perm = B.stack("perm")
    B.mksem("qSet", "qWm", "qWpa", "qWpb", "qAa", "qAb", "qWs", "qRow",
            "qX0", "qX1", "qX2", "qX3",
            "qSp", "qC", "qWin", "qW1a", "qW1b", "qW1c", "qW1d", "qW1e", "qW1f",
            "qW2a", "qW2b", "qW2c", "qW2d", "qGs", "qR", "qR2", "qR3",
            "qOs", "qHr", "qY", "qP", "qYa", "qYb", "qYc", "qYd", "qR4",
            "cT", "cS", "cV", "cG", "cc")
    sems, cnt = B.sems, B.cnt
    inc, incD, tok, wt = B.inc, B.incD, B.tok, B.wt
    bank_w = B.bank_w

    idt = B.sb("perm", "idts", [128, 128], BF16)
    bac = B.sb("perm", "bacs", [128, DC], F32)
    bbc = B.sb("perm", "bbcs", [128, DC], F32)
    bvc = B.sb("perm", "bvcs", [128, DC], F32)
    b1c = B.sb("perm", "b1cs", [128, HC], F32)
    a1c = B.sb("perm", "a1cs", [128, HC], F32)
    a2c8 = B.sb("perm", "a2c8s", [128, DC], F32)
    a2cb = B.sb("perm", "a2cbs", [128, DC], BF16)
    a2row8 = B.sb("perm", "a2row8s", [8, 128], F32)
    b2r = B.sb("perm", "b2rs", [1, D], F32)
    b2p = B.sb("perm", "b2ps", [1, D], BF16)
    a2acc = B.sb("perm", "a2accs", [1, D], F32)
    a2rb = B.sb("perm", "a2rbs", [1, D], BF16)
    a2bc = B.sb("perm", "a2bcs", [128, D], BF16)
    ones_u = B.sb("perm", "ones_us", [1, 128], BF16)
    rtmp = B.sb("perm", "rtmps", [128, 2], F32)
    c15 = B.sb("perm", "c15s", [128, GR], F32)
    ch = B.sb("perm", "chs", [128, GR], F32)
    ceps = B.sb("perm", "cepss", [128, GR], F32)
    mask = B.sb("perm", "masks", [128, NCORE], F32)
    gat = B.sb("perm", "gats", [128, NCORE, 16], F32)
    Hin = B.sb("perm", "Hins", [128, DC], F32)
    t1c = B.sb("perm", "t1cs", [128, DC], F32)
    P_run = B.sb("perm", "P_runs", [128, DC], F32)
    S_run = B.sb("perm", "S_runs", [128, DC], F32)
    pub = B.sb("perm", "pubs", [128, 16], F32)
    mv = B.sb("perm", "mvs", [128, 2, GR, 2], F32)
    bn6 = B.sb("perm", "bn6s", [128, 12], F32)
    wvar = B.sb("perm", "wvars", [128, 2, GR], F32)
    rva = B.sb("perm", "rvas", [128, 2, GR], F32)
    uva = B.sb("perm", "uvas", [128, 2, GR], F32)

    ps = [perm.enter_context(nc.psum_tensor(f"ps{i}", [128, 512], F32))
          for i in range(7)]

    block = perm.enter_context(nc.Block())
    T, V, Sc, Sy, G = (block.tensor, block.vector, block.scalar, block.sync,
                       block.gpsimd)

    esX = B.stack("X")
    xnf = B.sb("X", "xnf", [128, DC, TCORE], BF16)
    esA = B.stack("A")
    win_s = B.sb("A", "win_s", [128, DC, D], BF16)
    wa_s = B.sb("A", "wa_s", [128, DC, D], BF16)
    wb_s = B.sb("A", "wb_s", [128, DC, D], BF16)
    wvv_s = B.sb("A", "wvv_s", [128, DC, D], BF16)
    xfull = B.sb("A", "xfull", [128, GR, D], F32)
    xntm = B.sb("A", "xntm", [128, 2, D], BF16)
    xpf = B.sb("A", "xpf", [128, DC, TT], BF16)
    atmp = B.sb("A", "atmp", [128, 2, TT], F32)
    btmp = B.sb("A", "btmp", [128, 2, TT], F32)
    vtm2 = B.sb("A", "vtm2", [128, 2, TT], F32)
    hloc = B.sb("A", "hloc", [128, 4, TT], F32)
    ploc = B.sb("A", "ploc", [128, 4, TT], F32)
    pst = B.stacks["A"].enter_context(
        nc.psum_tensor("pst", [128, 1024], BF16))
    wfpb = B.sb("A", "wfpb", [128, 2, 2048], F32)
    aab = B.sb("A", "aab", [128, 2, 1024], F32)
    signb = B.sb("A", "signb", [128, 2, 2048], BF16)

    def setup_sy(sy):
        for dst, src in ((idt, idt_d), (bac, bac_d), (bbc, bbc_d),
                         (bvc, bvc_d), (b1c, b1c_d), (b2r, b2r_d),
                         (mask, mask_d)):
            i = sy.dma_start(out=dst[:], in_=src)
            incD(i, "qSet")
    Sy(setup_sy)
    setup_tok = tok("qSet")


    def setup_g(g):
        i = g.dma_start(out=win_s[:],
                        in_=winT.rearrange("(kc p) o -> p kc o", p=128))
        incD(i, "qWin")
        for t_, val in ((ones_u, 1.0), (S_run, 0.0), (P_run, 1.0),
                        (Hin, 0.0), (a2c8, 0.0), (c15, 1.5), (ch, 0.5),
                        (ceps, EPS)):
            i = g.memset(t_[:], val)
            inc(i, "cG")
        for dst, src in ((wa_s, waT), (wb_s, wbT), (wvv_s, wvT)):
            i = g.dma_start(out=dst[:],
                            in_=src.rearrange("(kc p) o -> p kc o", p=128))
            incD(i, "qWm")
    G(setup_g)
    mixw_tok = tok("qWin")
    abvw_tok = tok("qWm")
    memset_tok = tok("cG")

    st = {
        "signb_free": None, "sign_prev": None,
        "a1_done": None, "prep_done": None, "a2bc_tok": None, "b2p_tok": None,
        "wfp_free": [None, None], "aab_free": [None, None],
        "a1_prev": None, "a2_prev": None, "rt_free": [None, None],
    }

    def sign_w1_load(d, h, slot):
        def g_load(g):
            wt(g, "G", st["wfp_free"][slot])
            i = g.dma_start(out=wfpb[:, slot, :],
                            in_=w1T[d * 128:(d + 1) * 128,
                                    h * 2048:(h + 1) * 2048])
            incD(i, "qWpa" if slot == 0 else "qWpb")
        G(g_load)
        return tok("qWpa" if slot == 0 else "qWpb")

    def sign_w2_load(m, slot):
        def g_load(g):
            wt(g, "G", st["wfp_free"][slot])
            i = g.dma_start(
                out=wfpb[:, slot, :].rearrange("p (c d) -> p c d", c=2),
                in_=w2T[m * 256:(m + 1) * 256, :].rearrange(
                    "(c p) d -> p c d", p=128))
            incD(i, "qWpa" if slot == 0 else "qWpb")
        G(g_load)
        return tok("qWpa" if slot == 0 else "qWpb")

    def sign_compute_pair(items):
        lds = [it[4] for it in items]

        def s_sign(s):
            wt(s, "S", st["signb_free"], st["sign_prev"], *lds)
            for kind, a_, h, slot, ld in items:
                i = s.activation(out=signb[:, slot, :], in_=wfpb[:, slot, :],
                                 func=AF.Sign)
                inc(i, "cS")
        Sc(s_sign)
        sg_t = tok("cS")
        st["sign_prev"] = sg_t
        for kind, a_, h, slot, ld in items:
            st["wfp_free"][slot] = sg_t

        def g_store(g):
            wt(g, "G", sg_t)
            for kind, a_, h, slot, ld in items:
                if kind == "w1":
                    ci0 = h * 16
                    i = g.dma_start(
                        out=sw1d[ci0:ci0 + 16, :, a_, :].rearrange(
                            "ci p c -> p ci c"),
                        in_=signb[:, slot, :].rearrange(
                            "p (ci c) -> p ci c", ci=16))
                else:
                    i = g.dma_start(
                        out=sw2d[2 * a_:2 * a_ + 2, :, :].rearrange(
                            "ci p d -> p ci d"),
                        in_=signb[:, slot, :].rearrange(
                            "p (ci d) -> p ci d", ci=2))
                incD(i, "qWs")
        G(g_store)
        st["signb_free"] = tok("qWs")

    def alpha_load(kind, k, slot):
        def sy_load(sy):
            wt(sy, "Y", st["aab_free"][slot])
            if kind == "a1":
                i = sy.dma_start(out=aab[:, slot, :],
                                 in_=w1n[k * 128:(k + 1) * 128, :])
            else:
                dc, q = k // 4, k % 4
                i = sy.dma_start(
                    out=aab[:, slot, :],
                    in_=w2n[dc * 128:(dc + 1) * 128,
                            q * 1024:(q + 1) * 1024])
            incD(i, "qAa" if slot == 0 else "qAb")
        Sy(sy_load)
        return tok("qAa" if slot == 0 else "qAb")

    def alpha_compute(kind, k, slot, ld):
        if kind == "a1":
            def v_red(v):
                wt(v, "V", ld, st["a1_prev"])
                i = v.reduce_sum(out=a1c[:, k:k + 1],
                                 in_=aab[:, slot, :],
                                 axis=mybir.AxisListType.X,
                                 apply_absolute_value=True)
                inc(i, "cV")
            V(v_red)
            st["a1_prev"] = tok("cV")
            st["aab_free"][slot] = tok("cV")
            if k == HC - 1:
                def v_sc(v):
                    wt(v, "V", st["a1_prev"])
                    i = v.tensor_scalar(out=a1c[:], in0=a1c[:],
                                        scalar1=1.0 / D, scalar2=None,
                                        op0=ALU.mult)
                    inc(i, "cV")
                V(v_sc)
                st["a1_done"] = tok("cV")
        else:
            dc = k // 4
            rs = k % 2

            def v_red(v, dc=dc, rs=rs):
                wt(v, "V", ld, st["rt_free"][rs])
                i = v.reduce_sum(out=rtmp[:, rs:rs + 1], in_=aab[:, slot, :],
                                 axis=mybir.AxisListType.X,
                                 apply_absolute_value=True)
                inc(i, "cV")
                wt(v, "V", tok("cV"), st["a2_prev"])
                i = v.tensor_add(a2c8[:, dc:dc + 1], a2c8[:, dc:dc + 1],
                                 rtmp[:, rs:rs + 1])
                inc(i, "cV")
            V(v_red)
            st["a2_prev"] = tok("cV")
            st["rt_free"][rs] = tok("cV")
            st["aab_free"][slot] = tok("cV")

    def alpha2_finalize():
        def v_sc(v):
            wt(v, "V", st["a2_prev"])
            i = v.tensor_scalar(out=a2c8[:], in0=a2c8[:], scalar1=1.0 / HID,
                                scalar2=None, op0=ALU.mult)
            inc(i, "cV")
        V(v_sc)
        sc_t = tok("cV")

        def s_cast(s):
            wt(s, "S", sc_t)
            i = s.copy(out=a2cb[:], in_=a2c8[:])
            inc(i, "cS")
        Sc(s_cast)
        cb_t = tok("cS")

        def t_tr(t):
            wt(t, "T", cb_t, A["pst_free"])
            i = t.transpose(pst[0:8, 0:128], a2cb[:], idt[:])
            inc(i, "cT")
        T(t_tr)
        tr_t = tok("cT")

        def s_row(s):
            wt(s, "S", tr_t)
            i = s.copy(out=a2row8[:], in_=pst[0:8, 0:128])
            inc(i, "cS")
        Sc(s_row)
        A["pst_free"] = tok("cS")
        row_t = tok("cS")

        def sy_row(sy):
            wt(sy, "Y", row_t)
            for r in range(8):
                i = sy.dma_start(out=a2acc[0:1, r * 128:(r + 1) * 128],
                                 in_=a2row8[r:r + 1, :])
                incD(i, "qRow")
        Sy(sy_row)
        acc_t = tok("qRow")

        def s_rb(s):
            wt(s, "S", acc_t)
            i = s.copy(out=a2rb[:], in_=a2acc[:])
            inc(i, "cS")
        Sc(s_rb)
        rb_t = tok("cS")

        def v_b2p(v):
            wt(v, "V", rb_t)
            i = v.reciprocal(out=a2acc[:], in_=a2acc[:])
            inc(i, "cV")
            wt(v, "V", tok("cV"))
            i = v.tensor_mul(a2acc[:], a2acc[:], b2r[:])
            inc(i, "cV")
        V(v_b2p)
        bp_t = tok("cV")

        def s_b2p(s):
            wt(s, "S", bp_t)
            i = s.copy(out=b2p[:], in_=a2acc[:])
            inc(i, "cS")
        Sc(s_b2p)
        st["b2p_tok"] = tok("cS")

        for sg in range(2):
            def t_bc(t, sg=sg):
                wt(t, "T", tok("cS"), bank_w[6])
                i = t.matmul(ps[6][:, :], ones_u[:],
                             a2rb[0:1, sg * 512:(sg + 1) * 512],
                             start=True, stop=True)
                inc(i, "cT")
            T(t_bc)
            bc_t = tok("cT")

            def s_bc(s, sg=sg, bc_t=bc_t):
                wt(s, "S", bc_t)
                i = s.copy(out=a2bc[:, sg * 512:(sg + 1) * 512],
                           in_=ps[6][:, :])
                inc(i, "cS")
            Sc(s_bc)
            bank_w[6] = tok("cS")
        st["a2bc_tok"] = tok("cS")
        st["prep_done"] = tok("qWs")

    A = {
        "apply_done": [None, None], "trans_done": [None, None],
        "stat_done": [None, None], "stats_last": [None, None],
        "sgt_free": [None, None], "abv_free": [None, None],
        "vtmp_free": None, "spill": [None] * NTILE,
        "scan_last": None, "rot": 0, "spill_h": [None, None],
        "apply_prev": [None, None], "sgt_prev": [None, None],
        "apply_g": [None] * GR, "x_ld": [None] * GR,
        "xp_e": [None] * DC,
        "sqrt_done": [None, None],
        "recip_prev": [None, None], "rsqrt_tok": None,
        "abv_prev": [None, None], "xpf_free": None,
        "xnf_tok": [None] * NTILE, "pst_free": None,
    }

    def a_ln_load(j, g_):
        gi = j * GR + g_

        def sy_load(sy, g_=g_, gi=gi):
            wt(sy, "Y", A["apply_g"][g_])
            i = sy.dma_start(out=xfull[:, g_, :],
                             in_=x_d[gi * 128:(gi + 1) * 128, :])
            incD(i, "qX%d" % g_)
        Sy(sy_load)
        A["x_ld"][g_] = tok("qX%d" % g_)

    def a_ln_stats(j, g_):
        tp = j % 2
        ld = A["x_ld"][g_]

        def v_stats(v, g_=g_, tp=tp, ld=ld):
            wt(v, "V", ld, memset_tok, A["sqrt_done"][tp])
            wt(v, "V", tok("cV"))
            v.bn_stats(out=bn6[:, 0:6], in_=xfull[:, g_, 0:512])
            i = v.bn_stats(out=bn6[:, 6:12], in_=xfull[:, g_, 512:1024])
            inc(i, "cV")
            wt(v, "V", tok("cV"))
            i = v.bn_aggr(out=mv[:, tp, g_, :], in_=bn6[:])
            inc(i, "cV")
        V(v_stats)
        A["stats_last"][tp] = tok("cV")

    def a_ln_rsqrt(j):
        tp = j % 2
        stats_last = A["stats_last"][tp]

        def s_sqv(s, tp=tp, stats_last=stats_last):
            wt(s, "S", stats_last, memset_tok, A["recip_prev"][tp])
            i = s.activation(out=uva[:, tp, :], in_=mv[:, tp, :, 1],
                             func=AF.Sqrt, bias=ceps[:, 0:1])
            inc(i, "cS")
        Sc(s_sqv)
        A["sqrt_done"][tp] = tok("cS")

        def v_rec(v, tp=tp):
            wt(v, "V", A["sqrt_done"][tp], tok("cV"))
            i = v.reciprocal(rva[:, tp, :], uva[:, tp, :])
            inc(i, "cV")
        V(v_rec)
        A["rsqrt_tok"] = tok("cV")
        A["recip_prev"][tp] = A["rsqrt_tok"]

    def a_ln_apply(j):
        tp = j % 2
        last_ev = None
        for g_ in range(GR):
            sl = g_ % 2

            def v_apply(v, g_=g_, sl=sl, tp=tp):
                wt(v, "V", A["rsqrt_tok"], A["trans_done"][sl],
                   A["apply_prev"][sl])
                i = v.tensor_scalar(out=xntm[:, sl, :], in0=xfull[:, g_, :],
                                    scalar1=mv[:, tp, g_, 0:1],
                                    scalar2=rva[:, tp, g_:g_ + 1],
                                    op0=ALU.subtract, op1=ALU.mult)
                inc(i, "cV")
            V(v_apply)
            ap_t = tok("cV")
            A["apply_prev"][sl] = ap_t
            A["apply_done"][sl] = ap_t
            A["apply_g"][g_] = ap_t

            def t_trans(t, sl=sl, ap_t=ap_t):
                wt(t, "T", ap_t, A["pst_free"], setup_tok)
                i = None
                for kc in range(DC):
                    i = t.transpose(
                        pst[:, kc * 128:(kc + 1) * 128],
                        xntm[:, sl, kc * 128:(kc + 1) * 128], idt[:])
                inc(i, "cT")
            T(t_trans)
            tr_t = tok("cT")

            def v_evac(v, j=j, g_=g_, tr_t=tr_t):
                wt(v, "V", tr_t)
                off = j * TT + g_ * 128
                i = v.tensor_copy(xnf[:, :, off:off + 128],
                                  pst[:].rearrange("p (k c) -> p k c", k=DC))
                inc(i, "cV")
            V(v_evac)
            A["pst_free"] = tok("cV")
            A["trans_done"][sl] = tok("cT")
            last_ev = tok("cV")
        A["xnf_tok"][j] = last_ev

    def a_mm(j, preps=(), lnj=None):
        preps = list(preps)

        def run_prep():
            if preps:
                preps.pop(0)()

        xnf_tok = A["xnf_tok"][j]
        for ep in range(DC // 2):
            mm_ts = []
            for e in (2 * ep, 2 * ep + 1):
                bank = e % 7

                def t_xp(t, e=e, bank=bank, j=j, xnf_tok=xnf_tok):
                    wt(t, "T", bank_w[bank], xnf_tok, mixw_tok)
                    i = None
                    for kc in range(DC):
                        i = t.matmul(ps[bank][:],
                                     win_s[:, kc, e * 128:(e + 1) * 128],
                                     xnf[:, kc, j * TT:(j + 1) * TT],
                                     start=(kc == 0), stop=(kc == DC - 1))
                    inc(i, "cT")
                T(t_xp)
                mm_ts.append(tok("cT"))

            def s_silu(s, ep=ep, mm_ts=mm_ts):
                wt(s, "S", mm_ts[0], A["xpf_free"], A["sgt_prev"][1])
                for ix, e in enumerate((2 * ep, 2 * ep + 1)):
                    if ix == 1:
                        wt(s, "S", mm_ts[1])
                    i = s.activation(out=xpf[:, e, :],
                                     in_=ps[e % 7][:], func=AF.Silu)
                    inc(i, "cS")
                    bank_w[e % 7] = tok("cS")
                    A["xp_e"][e] = tok("cS")
            Sc(s_silu)
            A["sgt_prev"][1] = tok("cS")
            run_prep()

        if lnj is not None:
            for g_ in range(GR):
                a_ln_load(lnj, g_)
            for g_ in range(GR):
                a_ln_stats(lnj, g_)

        for fp in range(DC // 2):
            fbanks = []
            mmts = []
            for f in (2 * fp, 2 * fp + 1):
                ba = A["rot"] % 7
                bb_ = (A["rot"] + 1) % 7
                bv_ = (A["rot"] + 2) % 7
                A["rot"] += 3
                fbanks.append((ba, bb_, bv_))

                def t_abv(t, f=f, ba=ba, bb_=bb_, bv_=bv_):
                    wt(t, "T", abvw_tok)
                    for wsb, bank in ((wa_s, ba), (wb_s, bb_), (wvv_s, bv_)):
                        wt(t, "T", bank_w[bank])
                        i = None
                        for e in range(DC):
                            wt(t, "T", A["xp_e"][e])
                            i = t.matmul(ps[bank][:],
                                         wsb[:, e, f * 128:(f + 1) * 128],
                                         xpf[:, e, :],
                                         start=(e == 0), stop=(e == DC - 1))
                        inc(i, "cT")
                T(t_abv)
                mmts.append(tok("cT"))

            def s_abv(s, fp=fp, fbanks=fbanks, mmts=mmts):
                abv2 = {}
                for sl in range(2):
                    f = 2 * fp + sl
                    ba, bb_, bv_ = fbanks[sl]
                    wt(s, "S", mmts[sl], A["abv_free"][sl],
                       A["abv_prev"][sl])
                    i = s.activation(out=atmp[:, sl, :], in_=ps[ba][:],
                                     func=AF.Sigmoid, bias=bac[:, f:f + 1])
                    inc(i, "cS")
                    bank_w[ba] = tok("cS")
                    i = s.activation(out=btmp[:, sl, :], in_=ps[bb_][:],
                                     func=AF.Sigmoid, bias=bbc[:, f:f + 1])
                    inc(i, "cS")
                    bank_w[bb_] = tok("cS")
                    i = s.activation(out=vtm2[:, sl, :], in_=ps[bv_][:],
                                     func=AF.Tanh, bias=bvc[:, f:f + 1])
                    inc(i, "cS")
                    bank_w[bv_] = tok("cS")
                    abv2[sl] = tok("cS")
                    A["abv_prev"][sl] = abv2[sl]
                A["abv2"] = abv2
            Sc(s_abv)
            abv2 = A["abv2"]

            for f in (2 * fp, 2 * fp + 1):
                slot = f % 2
                abv_t = abv2[slot]

                def v_scan(v, f=f, slot=slot, abv_t=abv_t):
                    wt(v, "V", abv_t, A["spill_h"][1 - f // 4])
                    i = v.tensor_mul(btmp[:, slot, :], btmp[:, slot, :],
                                     vtm2[:, slot, :])
                    inc(i, "cV")
                    wt(v, "V", tok("cV"))
                    i = v.tensor_tensor_scan(out=hloc[:, f % 4, :],
                                             data0=atmp[:, slot, :],
                                             data1=btmp[:, slot, :],
                                             initial=S_run[:, f:f + 1],
                                             op0=ALU.mult, op1=ALU.add)
                    inc(i, "cV")
                    i = v.tensor_tensor_scan(out=ploc[:, f % 4, :],
                                             data0=atmp[:, slot, :],
                                             data1=atmp[:, slot, :],
                                             initial=P_run[:, f:f + 1],
                                             op0=ALU.mult, op1=ALU.min)
                    inc(i, "cV")
                V(v_scan)
                A["abv_free"][slot] = tok("cV")
                if f % 4 == 3:
                    hh = f // 4

                    def v_sth(v, hh=hh):
                        wt(v, "V", tok("cV"))
                        i = v.tensor_copy(S_run[:, hh * 4:hh * 4 + 4],
                                          hloc[:, :, TT - 1])
                        inc(i, "cV")
                        i = v.tensor_copy(P_run[:, hh * 4:hh * 4 + 4],
                                          ploc[:, :, TT - 1])
                        inc(i, "cV")
                    V(v_sth)
                    sc = tok("cV")
                    if hh == 1:
                        A["scan_last"] = sc

                    def g_spill(g, j=j, hh=hh, sc=sc):
                        wt(g, "G", sc)
                        i = g.dma_start(
                            out=hsp[j, :, hh * 2048:(hh + 1) * 2048],
                            in_=hloc[:].rearrange("p a b -> p (a b)"))
                        incD(i, "qSp")
                        i = g.dma_start(
                            out=psp[j, :, hh * 2048:(hh + 1) * 2048],
                            in_=ploc[:].rearrange("p a b -> p (a b)"))
                        incD(i, "qSp")
                    G(g_spill)
                    A["spill_h"][hh] = tok("qSp")
                    if hh == 1:
                        A["spill"][j] = A["spill_h"][hh]
            if lnj is not None and fp == 1:
                a_ln_rsqrt(lnj)
            run_prep()
            run_prep()

        A["xpf_free"] = tok("cT")

        if lnj is not None:
            a_ln_apply(lnj)

    sign_slices = [("w1", d, h) for d in range(DC) for h in range(2)] + \
                  [("w2", m, 0) for m in range(16)]
    alpha_slices = [("a1", k) for k in range(HC)] + \
                   [("a2", k) for k in range(32)]
    pstate = {"sp": [], "ap": []}

    def prep_point(k):
        sp, ap = pstate["sp"], pstate["ap"]
        if len(sp) >= 2 and (k % 2 == 0 or k >= 32):
            items = [sp.pop(0), sp.pop(0)]
            sign_compute_pair(items)
        elif k >= 32 and sp:
            sign_compute_pair([sp.pop(0)])
        if k < 32:
            kind, a_, h = sign_slices[k]
            slot = k % 2
            ld = (sign_w1_load(a_, h, slot) if kind == "w1"
                  else sign_w2_load(a_, slot))
            sp.append((kind, a_, h, slot, ld))
        for u in range(2):
            if len(ap) >= 2 or (k >= 32 and ap):
                kind, k_, slot, ld = ap.pop(0)
                alpha_compute(kind, k_, slot, ld)
            ku = 2 * k + u
            if ku < 64:
                kind, k_ = alpha_slices[ku]
                slot = ku % 2
                ld = alpha_load(kind, k_, slot)
                ap.append((kind, k_, slot, ld))

    for g_ in range(GR):
        a_ln_load(0, g_)
    for g_ in range(GR):
        a_ln_stats(0, g_)
    a_ln_rsqrt(0)
    a_ln_apply(0)
    pp_off = [0, 11, 22, 32, 32]
    for j in range(NTILE):
        a_mm(j, [(lambda k: (lambda: prep_point(k)))(pp_off[j] + q)
                 for q in range(pp_off[j + 1] - pp_off[j])],
             lnj=j + 1 if j + 1 < NTILE else None)
    k = 32
    while pstate["sp"] or pstate["ap"]:
        prep_point(k)
        k += 1
    alpha2_finalize()

    def v_pub(v):
        wt(v, "V", A["scan_last"])
        i = v.tensor_copy(pub[:, 0:8], P_run[:])
        inc(i, "cV")
        i = v.tensor_copy(pub[:, 8:16], S_run[:])
        inc(i, "cV")
    V(v_pub)
    pub_tok = tok("cV")

    def sy_pub(sy):
        wt(sy, "Y", pub_tok)
        i = sy.dma_start(out=ccin.ap(), in_=pub[:])
        incD(i, "qC")
    Sy(sy_pub)
    ccin_tok = tok("qC")

    def g_cc(g):
        wt(g, "G", ccin_tok)
        i = g.collective_compute(
            "AllGather", ALU.bypass,
            replica_groups=[list(range(NCORE))],
            ins=[ccin.ap()], outs=[ccout.ap()])
        inc(i, "cc")
    G(g_cc)
    cc_tok = tok("cc")

    def sy_gat(sy):
        wt(sy, "Y", cc_tok)
        i = sy.dma_start(out=gat[:],
                         in_=ccout.ap().rearrange("(r p) c -> p r c", p=128))
        incD(i, "qC")
    Sy(sy_gat)
    gat_tok = tok("qC")

    def v_compose(v):
        wt(v, "V", gat_tok, memset_tok, setup_tok)
        for jr in range(NCORE):
            wt(v, "V", tok("cV"))
            i = v.tensor_mul(t1c[:], gat[:, jr, 0:8], Hin[:])
            inc(i, "cV")
            wt(v, "V", tok("cV"))
            i = v.tensor_add(t1c[:], t1c[:], gat[:, jr, 8:16])
            inc(i, "cV")
            wt(v, "V", tok("cV"))
            i = v.tensor_sub(t1c[:], t1c[:], Hin[:])
            inc(i, "cV")
            wt(v, "V", tok("cV"))
            i = v.scalar_tensor_tensor(out=Hin[:], in0=t1c[:],
                                       scalar=mask[:, jr:jr + 1], in1=Hin[:],
                                       op0=ALU.mult, op1=ALU.add)
            inc(i, "cV")
    V(v_compose)
    hin_tok = tok("cV")

    def setup_pro(s):
        i = s.dma_start(out=y_d[:], in_=x_d[:])
        incD(i, "qP")
    Sc(setup_pro)
    pro_tok = tok("qP")

    aend = (A["spill"][NTILE - 1], A["scan_last"], tok("cT"), tok("cS"),
            st["prep_done"])
    esA.close()
    esH = B.stack("H")
    ps.append(esH.enter_context(nc.psum_tensor("ps7", [128, 512], F32)))
    h1 = B.sb("H", "h1", [128, HC, TCORE], BF16)
    esC = B.stack("C")
    hlo = B.sb("C", "hlo", [128, DC, 256], F16)
    plo = B.sb("C", "plo", [128, DC, 256], F16)
    hht = B.sb("C", "hht", [128, 256], F32)
    otst = B.sb("C", "otst", [128, DC, 256], BF16)

    C = {"hp_free": None, "ot_free": None, "th_prev": None, "hh_free": None}

    def corr_unit(u):
        j, hh = u // 2, u % 2
        c0 = hh * 256

        def sy_rel(sy, j=j, c0=c0, u=u):
            wt(sy, "Y", C["hp_free"], tok("qSp"))
            if u == 0:
                wt(sy, "Y", st["prep_done"], st["sign_prev"],
                   st["a1_prev"], st["a2_prev"], A["scan_last"])
            i = sy.dma_start(
                out=hlo[:],
                in_=hsp[j, :, :].rearrange("p (f t) -> p f t", f=DC)[
                    :, :, c0:c0 + 256])
            incD(i, "qHr")
            i = sy.dma_start(
                out=plo[:],
                in_=psp[j, :, :].rearrange("p (f t) -> p f t", f=DC)[
                    :, :, c0:c0 + 256])
            incD(i, "qHr")
        Sy(sy_rel)
        hp_t = tok("qHr")

        def v_corr(v, hp_t=hp_t):
            wt(v, "V", hp_t, hin_tok, C["hh_free"])
            for f in range(DC):
                i = v.scalar_tensor_tensor(out=hlo[:, f, :], in0=plo[:, f, :],
                                           scalar=Hin[:, f:f + 1],
                                           in1=hlo[:, f, :],
                                           op0=ALU.mult, op1=ALU.add)
                inc(i, "cV")
        V(v_corr)
        corr_t = tok("cV")
        C["hp_free"] = corr_t

        def s_th(s, corr_t=corr_t):
            wt(s, "S", corr_t, C["th_prev"], C["ot_free"])
            for f in range(DC):
                i = s.activation(out=otst[:, f, :], in_=hlo[:, f, :],
                                 func=AF.Tanh, scale=0.5)
                inc(i, "cS")
        Sc(s_th)
        th_t = tok("cS")
        C["th_prev"] = th_t

        def v_o(v, th_t=th_t):
            wt(v, "V", th_t)
            for f in range(DC):
                wt(v, "V", tok("cV"))
                i = v.tensor_mul(hht[:], hlo[:, f, :], otst[:, f, :])
                inc(i, "cV")
                wt(v, "V", tok("cV"))
                i = v.tensor_add(hht[:], hht[:], hlo[:, f, :])
                inc(i, "cV")
                wt(v, "V", tok("cV"))
                i = v.scalar_tensor_tensor(out=otst[:, f, :],
                                           in0=hht[:], scalar=0.5,
                                           in1=hlo[:, f, :],
                                           op0=ALU.mult, op1=ALU.mult)
                inc(i, "cV")
        V(v_o)
        C["hh_free"] = tok("cV")
        o_done = tok("cV")
        o_done = tok("cV")

        def sy_osp(sy, j=j, c0=c0, o_done=o_done):
            wt(sy, "Y", o_done)
            i = sy.dma_start(
                out=osp[j, :, :].rearrange("p (f t) -> p f t", f=DC)[
                    :, :, c0:c0 + 256],
                in_=otst[:])
            incD(i, "qOs")
        Sy(sy_osp)
        C["ot_free"] = tok("qOs")
    esS1 = B.stack("S1")
    sw1r = B.sb("S1", "sw1r", [128, 5, DC, 128], BF16)

    sw1_free = [None] * 5
    W1Q = ["qW1a", "qW1b", "qW1c", "qW1d", "qW1e"]
    gelu_toks = []
    for ci in range(HC):
        if ci % 4 == 0:
            corr_unit(ci // 4)
        slot = ci % 5

        def g_ld(g, ci=ci, slot=slot):
            if ci < 5:
                wt(g, "G", *aend)
            wt(g, "G", sw1_free[slot], st["prep_done"])
            i = g.dma_start(out=sw1r[:, slot, :, :], in_=sw1d[ci, :, :, :])
            incD(i, W1Q[slot])
        G(g_ld)
        ld = tok(W1Q[slot])

        def t_h1(t, ci=ci, slot=slot, ld=ld):
            wt(t, "T", ld, st["a1_done"])
            for jt in range(NTILE):
                wt(t, "T", bank_w[(ci % 2) * 4 + jt])
            i = None
            for d in range(DC):
                for jt in range(NTILE):
                    i = t.matmul(ps[(ci % 2) * 4 + jt][:],
                                 sw1r[:, slot, d, :],
                                 xnf[:, d, jt * TT:(jt + 1) * TT],
                                 start=(d == 0), stop=(d == DC - 1))
                    if d == DC - 1:
                        inc(i, "cT")
        T(t_h1)
        mm_end = cnt["cT"]

        def s_gelu(s, ci=ci, mm_end=mm_end):
            for jt in range(NTILE):
                wt(s, "S", ("cT", mm_end - (NTILE - 1 - jt)))
                i = s.activation(out=h1[:, ci, jt * TT:(jt + 1) * TT],
                                 in_=ps[(ci % 2) * 4 + jt][:], func=AF.Gelu,
                                 scale=a1c[:, ci:ci + 1], bias=b1c[:, ci:ci + 1])
                inc(i, "cS")
                bank_w[(ci % 2) * 4 + jt] = tok("cS")
        Sc(s_gelu)
        sw1_free[slot] = tok("cS")
        gelu_toks.append(tok("cS"))
    h1_tok = gelu_toks[-1]

    b1a_end = tok("cT")
    esS1.close()
    esC.close()
    corr_done = (C["ot_free"], tok("cG"))
    esS2 = B.stack("S2")
    sw2r = B.sb("S2", "sw2r", [128, 4, D], BF16)
    gst = B.sb("S2", "gst", [128, GR, D], F32)

    sw2_free = [(b1a_end, *corr_done)] * 4
    sw2_free = [b1a_end] * 4
    cd0, cd1 = corr_done
    W2Q = ["qW2a", "qW2b", "qW2c", "qW2d"]
    gst_free = None

    gt2b = xnf.bitcast(F16).reshape([128, 4, GR, D])
    rel2_tok = [None] * NTILE
    g_done = [None, None]

    def b2_rel2(j, war):
        sl = j % 2
        q = "qR2" if sl == 0 else "qR4"

        def sy_rel2(sy, j=j, sl=sl, q=q, war=war):
            wt(sy, "Y", b1a_end, war)
            i = sy.dma_start(out=gt2b[:, sl, :, :],
                             in_=gsp[j * GR:(j + 1) * GR, :, :].rearrange(
                                 "t p c -> p t c"))
            incD(i, q)
        Sy(sy_rel2)
        rel2_tok[j] = tok(q)
    for tgg in range(4):
        def t_ones(t, tgg=tgg):
            wt(t, "T", st["b2p_tok"], h1_tok)
            for k in range(8):
                wt(t, "T", bank_w[k])
            for tg4 in range(4):
                for hf in range(2):
                    t.matmul(ps[tg4 * 2 + hf][:], ones_u[:],
                             b2p[0:1, hf * 512:(hf + 1) * 512],
                             start=True, stop=False)
        T(t_ones)

        for ci in range(HC):
            slot = ci % 4

            def g_ld(g, ci=ci, slot=slot):
                wt(g, "G", sw2_free[slot], cd0, cd1)
                i = g.dma_start(out=sw2r[:, slot, :], in_=sw2d[ci, :, :])
                incD(i, W2Q[slot])
            G(g_ld)
            ld = tok(W2Q[slot])

            def t_w2(t, tgg=tgg, ci=ci, slot=slot, ld=ld):
                wt(t, "T", ld)
                i = None
                for tg4 in range(4):
                    tg = tgg * 4 + tg4
                    for hf in range(2):
                        i = t.matmul(
                            ps[tg4 * 2 + hf][:],
                            h1[:, ci, tg * 128:(tg + 1) * 128],
                            sw2r[:, slot, hf * 512:(hf + 1) * 512],
                            start=False, stop=(ci == HC - 1))
                        if ci == HC - 1 or (tg4 == 3 and hf == 1):
                            inc(i, "cT")
            T(t_w2)
            sw2_free[slot] = tok("cT")
        w2_end = cnt["cT"]

        def v_gev(v, tgg=tgg, w2_end=w2_end, gf=gst_free):
            wt(v, "V", ("cT", w2_end), st["a2bc_tok"], gf)
            for tg4 in range(4):
                for hf in range(2):
                    i = v.tensor_mul(gst[:, tg4, hf * 512:(hf + 1) * 512],
                                     ps[tg4 * 2 + hf][:],
                                     a2bc[:, hf * 512:(hf + 1) * 512])
                    inc(i, "cV")
                    bank_w[tg4 * 2 + hf] = tok("cV")
        V(v_gev)
        gev_t = tok("cV")

        def g_gsp(g, tgg=tgg, gev_t=gev_t):
            wt(g, "G", gev_t)
            for tg4 in range(4):
                i = g.dma_start(out=gsp[tgg * 4 + tg4, :, :],
                                 in_=gst[:, tg4, :])
                incD(i, "qGs")
        G(g_gsp)
        gst_free = tok("qGs")
        if tgg < 2:
            b2_rel2(tgg, gst_free)

    b1b_end = tok("cT")
    esS2.close()
    esH.close()
    esB2 = B.stack("B2")
    wo_s = B.sb("B2", "wo_s", [128, DC, D], BF16)

    def g_wo(g):
        wt(g, "G", tok("cT"))
        i = g.dma_start(out=wo_s[:],
                        in_=woT.rearrange("(kc p) o -> p kc o", p=128))
        incD(i, "qWm")
    G(g_wo)
    wo_tok = tok("qWm")
    otb = B.sb("B2", "otb", [128, 2, DC, TT], BF16)
    gs2 = B.sb("B2", "gs2", [128, 2, 512], F32)
    sv2 = B.sb("B2", "sv2", [128, 4, 2, 512], F32)

    b2tok = {
        "ot_free": [b1b_end, b1b_end],
        "g_free": gst_free,
        "gs_free": [b1b_end, b1b_end], "sv_free": [None, None],
        "gs_prev": b1b_end, "acc_done": [None] * 4,
    }
    ot_tok = [None] * NTILE
    final_toks = []

    def b2_rel(j):
        sl = j % 2
        q = "qR" if sl == 0 else "qR3"

        def sy_rel(sy, j=j, sl=sl, q=q):
            wt(sy, "Y", b2tok["ot_free"][sl])
            if j == 1:
                wt(sy, "Y", ot_tok[0][1])
            if j < 2:
                wt(sy, "Y", b2tok["g_free"])
                wt(sy, "Y", tok("qOs"))
            if j == 0:
                i = sy.dma_start(out=otb[:, sl, 0:4, :].rearrange(
                                     "p a b -> p (a b)"),
                                 in_=osp[j, :, 0:2048])
                incD(i, "qR")
                i = sy.dma_start(out=otb[:, sl, 4:8, :].rearrange(
                                     "p a b -> p (a b)"),
                                 in_=osp[j, :, 2048:4096])
                incD(i, "qR3")
            else:
                i = sy.dma_start(out=otb[:, sl, :, :].rearrange(
                                     "p a b -> p (a b)"),
                                 in_=osp[j, :, :])
                incD(i, q)
        Sy(sy_rel)
        if j == 0:
            ot_tok[j] = (tok("qR"), tok("qR3"))
        else:
            ot_tok[j] = (tok(q), tok(q))

    def b2_tile(j):
        sl = j % 2
        rel_t = rel2_tok[j]

        for tg4 in range(GR):
            tg = j * GR + tg4
            mm_ts = {}
            for hf in range(2):
                bank = (tg4 % 2) * 2 + hf

                def t_s(t, j=j, sl=sl, tg4=tg4, hf=hf, bank=bank):
                    wt(t, "T", ot_tok[j][0], wo_tok, bank_w[bank])
                    i = None
                    for f in range(DC):
                        if f == 4:
                            wt(t, "T", ot_tok[j][1])
                        i = t.matmul(ps[bank][:],
                                     otb[:, sl, f, tg4 * 128:(tg4 + 1) * 128],
                                     wo_s[:, f, hf * 512:(hf + 1) * 512],
                                     start=(f == 0), stop=(f == DC - 1))
                    inc(i, "cT")
                T(t_s)
                mm_ts[hf] = tok("cT")

            def s_gs(s, sl=sl, tg4=tg4, rel_t=rel_t):
                wt(s, "S", rel_t, b2tok["gs_free"][0], b2tok["gs_free"][1],
                   b2tok["gs_prev"])
                for hf in range(2):
                    i = s.activation(out=gs2[:, hf, :],
                                     in_=gt2b[:, sl, tg4,
                                              hf * 512:(hf + 1) * 512],
                                     func=AF.Sigmoid)
                    inc(i, "cS")
            Sc(s_gs)
            gs_t = tok("cS")
            b2tok["gs_prev"] = gs_t

            tpar = tg4 % 4
            for hf in range(2):
                bank = (tg4 % 2) * 2 + hf

                def v_fin(v, sl=sl, tg4=tg4, tpar=tpar, hf=hf, bank=bank,
                          s_mm=mm_ts[hf], gs_t=gs_t):
                    wt(v, "V", gs_t, s_mm, b2tok["acc_done"][tpar])
                    i = v.tensor_mul(gt2b[:, sl, tg4,
                                          hf * 512:(hf + 1) * 512],
                                     gt2b[:, sl, tg4,
                                          hf * 512:(hf + 1) * 512],
                                     gs2[:, hf, :])
                    inc(i, "cV")
                    b2tok["gs_free"][hf] = tok("cV")
                    wt(v, "V", tok("cV"))
                    i = v.tensor_mul(sv2[:, tpar, hf, :], ps[bank][:],
                                     gt2b[:, sl, tg4,
                                          hf * 512:(hf + 1) * 512])
                    inc(i, "cV")
                    bank_w[bank] = tok("cV")
                V(v_fin)

            fin_t = tok("cV")
            qacc = ("qYa", "qYb", "qYc", "qYd")[tpar]

            def g_acc(g, tg=tg, tpar=tpar, fin_t=fin_t, qacc=qacc):
                wt(g, "G", fin_t, pro_tok)
                i = g.dma_start(out=y_d[tg * 128:(tg + 1) * 128, :],
                                in_=sv2[:, tpar, :, :].rearrange(
                                    "p a b -> p (a b)"),
                                accum_op=ALU.add)
                incD(i, qacc)
            G(g_acc)
            b2tok["acc_done"][tpar] = tok(qacc)
            final_toks.append(tok(qacc))
        b2tok["ot_free"][sl] = tok("cT")
        b2tok["g_free"] = tok("cV")
        g_done[sl] = tok("cV")

    b2_rel(0)
    for j in range(NTILE):
        if j + 1 < NTILE:
            b2_rel(j + 1)
        b2_tile(j)
        if j + 2 < NTILE:
            b2_rel2(j + 2, g_done[j % 2])

    def sy_end(sy):
        wt(sy, "Y", pro_tok, *final_toks[-4:])
    Sy(sy_end)

    esB2.close()
    esX.close()
    B.stacks["perm"].close()
    nc._dbg_cnt = dict(cnt)
    return nc


_NC_CACHE = {}


def kernel(**inputs):
    x = np.ascontiguousarray(np.asarray(inputs["L0_attn_x"], dtype=np.float32))
    W_in = np.asarray(inputs["L0_attn_W_in"], dtype=np.float32)
    W_a = np.asarray(inputs["L0_attn_W_alpha"], dtype=np.float32)
    b_a = np.asarray(inputs["L0_attn_b_alpha"], dtype=np.float32)
    W_b = np.asarray(inputs["L0_attn_W_beta"], dtype=np.float32)
    b_b = np.asarray(inputs["L0_attn_b_beta"], dtype=np.float32)
    W_v = np.asarray(inputs["L0_attn_W_v"], dtype=np.float32)
    b_v = np.asarray(inputs["L0_attn_b_v"], dtype=np.float32)
    W_o = np.asarray(inputs["L0_attn_W_out"], dtype=np.float32)
    w1 = np.asarray(inputs["L0_ffn_w1"], dtype=np.float32)
    b1 = np.asarray(inputs["L0_ffn_b1"], dtype=np.float32)
    w2 = np.asarray(inputs["L0_ffn_w2"], dtype=np.float32)
    b2 = np.asarray(inputs["L0_ffn_b2"], dtype=np.float32)

    Bn, Tn, Dn = x.shape
    assert (Bn, Tn, Dn) == (2, 8192, 1024)
    x2d = x.reshape(Bn * Tn, Dn)

    shared = {
        "winT": np.ascontiguousarray(W_in.T),
        "waT": np.ascontiguousarray(W_a.T),
        "wbT": np.ascontiguousarray(W_b.T),
        "wvT": np.ascontiguousarray(W_v.T),
        "woT": np.ascontiguousarray(W_o.T),
        "w1T": np.ascontiguousarray(w1.T),
        "w2T": np.ascontiguousarray(w2.T),
        "bac": np.ascontiguousarray(b_a.reshape(DC, 128).T),
        "bbc": np.ascontiguousarray(b_b.reshape(DC, 128).T),
        "bvc": np.ascontiguousarray(b_v.reshape(DC, 128).T),
        "b1c": np.ascontiguousarray(b1.reshape(HC, 128).T),
        "b2r": np.ascontiguousarray(b2.reshape(1, D)),
        "w1n": np.ascontiguousarray(w1),
        "w2n": np.ascontiguousarray(w2),
    }
    import ml_dtypes
    shared["idt"] = np.eye(128, dtype=ml_dtypes.bfloat16)

    if "nc" not in _NC_CACHE:
        _NC_CACHE["nc"] = build_nc()
    nc = _NC_CACHE["nc"]

    in_maps = []
    for k in range(NCORE):
        grp = k // 4
        m = np.zeros((128, NCORE), dtype=np.float32)
        for jr in range(NCORE):
            if jr // 4 == grp and jr < k:
                m[:, jr] = 1.0
        im = dict(shared)
        im["mask"] = m
        im["x"] = np.ascontiguousarray(x2d[k * TCORE:(k + 1) * TCORE])
        in_maps.append(im)

    res = run_bass_kernel_spmd(nc, in_maps, core_ids=list(range(NCORE)))
    out = np.concatenate([np.asarray(res.results[k]["y"])
                          for k in range(NCORE)], axis=0)
    return out.reshape(Bn, Tn, Dn).astype(np.float32)
